# version 22
# baseline (speedup 1.0000x reference)
import contextlib
import numpy as np
import ml_dtypes
import concourse.bass as bass
import concourse.mybir as mybir
from concourse.bass_utils import run_bass_kernel_spmd

F32 = mybir.dt.float32
BF16 = mybir.dt.bfloat16
AF = mybir.ActivationFunctionType
ALU = mybir.AluOpType
AX = mybir.AxisListType
BF = ml_dtypes.bfloat16

D = 2048
DK = 128
H = 16
S = 8192
T = 2048
NB = 2
NCORE = 8
NCHUNK = 4
QC = 512
EPS = 1e-5
NEXP = 8
CAP = 1024
SEGS = [(o, min(512, CAP - o)) for o in range(0, CAP, 512)]
MOE = "sparse"
SCALE = float(1.0 / np.sqrt(DK))

ENGINES = ("sync", "tensor", "vector", "scalar", "gpsimd")


class Sem:
    __slots__ = ("name", "h", "n")

    def __init__(self, name):
        self.name, self.h, self.n = name, None, 0


class Eng:
    def __init__(self, prog, name, self_sem):
        self.prog, self.name, self.self_sem = prog, name, self_sem
        self.q = []

    def wait(self, sem, val=None):
        v = sem.n if val is None else val
        if v > 0:
            self.q.append(("w", sem, v))

    def op(self, opname, *args, inc=None, chain=True, **kw):
        incs = []
        if chain and self.self_sem is not None:
            self.wait(self.self_sem)
            self.self_sem.n += 1
            incs.append((self.self_sem, 1))
        if inc is not None:
            sem, amt = inc
            sem.n += amt
            incs.append((sem, amt))
        self.q.append(("o", opname, args, kw, incs))

    def raw(self, fn):
        self.q.append(("c", fn))

    def dma(self, out, in_, sem, **kw):
        sem.n += 16
        self.q.append(("o", "dma_start", (), dict(out=out, in_=in_, **kw),
                       [(sem, 16)]))
        return sem.n

    def replay(self, eng):
        for it in self.q:
            if it[0] == "w":
                eng.wait_ge(it[1].h, it[2])
            elif it[0] == "c":
                it[1](eng)
            else:
                _, opname, args, kw, incs = it
                kw = {k: (v() if callable(v) and getattr(v, "_late", False) else v)
                      for k, v in kw.items()}
                ins = getattr(eng, opname)(*args, **kw)
                for sem, amt in incs:
                    ins.then_inc(sem.h, amt)


class Prog:
    def __init__(self):
        self.sems = {}
        self.engs = {}
        for e in ENGINES:
            ss = self._sem("chain_" + e) if e in ("vector", "scalar", "gpsimd") else None
            self.engs[e] = Eng(self, e, ss)

    def _sem(self, name):
        if name not in self.sems:
            self.sems[name] = Sem(name)
        return self.sems[name]


def build_nc(phases=("p1", "p2", "p3", "p4"), dbg=()):
    nc = bass.Bass("TRN2", target_bir_lowering=False, debug=False)
    ap = lambda t: t.ap()

    xq = ap(nc.dram_tensor("xq", [T, D], F32, kind="ExternalInput"))
    xkv = ap(nc.dram_tensor("xkv", [S, D], F32, kind="ExternalInput"))
    masks = ap(nc.dram_tensor("masks", [16, 128, QC], BF16, kind="ExternalInput"))
    WqT = ap(nc.dram_tensor("WqT", [D, D], F32, kind="ExternalInput"))
    WkT = ap(nc.dram_tensor("WkT", [D, D], F32, kind="ExternalInput"))
    WvT = ap(nc.dram_tensor("WvT", [D, D], F32, kind="ExternalInput"))
    WoT = ap(nc.dram_tensor("WoT", [D, D], F32, kind="ExternalInput"))
    bq = ap(nc.dram_tensor("bq", [D], F32, kind="ExternalInput")).rearrange("(a b) -> a b", b=1)
    bk = ap(nc.dram_tensor("bk", [D], F32, kind="ExternalInput")).rearrange("(a b) -> a b", b=1)
    bv = ap(nc.dram_tensor("bv", [D], F32, kind="ExternalInput")).rearrange("(a b) -> a b", a=1)
    bo = ap(nc.dram_tensor("bo", [D], F32, kind="ExternalInput")).rearrange("(a b) -> a b", a=1)
    Wr = ap(nc.dram_tensor("Wr", [D, NEXP], F32, kind="ExternalInput"))
    Wg = ap(nc.dram_tensor("Wg", [NEXP, D, D], F32, kind="ExternalInput"))
    Wu = ap(nc.dram_tensor("Wu", [NEXP, D, D], F32, kind="ExternalInput"))
    Wd = ap(nc.dram_tensor("Wd", [NEXP, D, D], F32, kind="ExternalInput"))
    out = ap(nc.dram_tensor("out", [T + 128, D], F32, kind="ExternalOutput"))

    WgB = ap(nc.dram_tensor("WgB", [NEXP, D, D], BF16))
    WuB = ap(nc.dram_tensor("WuB", [NEXP, D, D], BF16))
    WdB = ap(nc.dram_tensor("WdB", [NEXP, D, D], BF16))
    H2TM = ap(nc.dram_tensor("H2TM", [T + 128, D], BF16))
    GATES_D = ap(nc.dram_tensor("GATES_D", [T + 128, NEXP], F32))
    IDXTOK = ap(nc.dram_tensor("IDXTOK", [NEXP * CAP + 128, 1], mybir.dt.int32))
    Kscr = ap(nc.dram_tensor("Kscr", [H, DK, S], BF16))
    Vscr = ap(nc.dram_tensor("Vscr", [H, 128, S // 128, DK], BF16))
    Qscr = ap(nc.dram_tensor("Qscr", [H, DK, T], BF16))
    CTX = ap(nc.dram_tensor("CTXscr", [H, DK, T], BF16))
    LN2F = ap(nc.dram_tensor("LN2F", [D // 128, 128, T], BF16))

    dbg_outs = []
    for name, src, shape in (
        ("kscr", Kscr, [H, DK, S]), ("vscr", Vscr, [H, 128, S // 128, DK]),
        ("qscr", Qscr, [H, DK, T]), ("ctx", CTX, [H, DK, T]),
        ("ln2f", LN2F, [D // 128, 128, T]),
    ):
        if name in dbg:
            d = ap(nc.dram_tensor("d_" + name, shape, BF16, kind="ExternalOutput"))
            dbg_outs.append((d, src))

    identc = ap(nc.inline_tensor(np.eye(128, dtype=BF), name="identc"))
    ltc = ap(nc.inline_tensor(np.tril(np.ones((128, 128), dtype=np.float32), -1).astype(BF), name="ltc"))
    tokidc = ap(nc.inline_tensor(
        (np.arange(128)[:, None] + 128 * np.arange(16)[None, :]).astype(np.int32), name="tokidc"))
    offsc = ap(nc.inline_tensor((CAP * np.arange(NEXP, dtype=np.float32))[None, :].copy(), name="offsc"))
    idxinitc = ap(nc.inline_tensor(
        (T + (np.arange(NEXP * CAP + 128) % 128)).astype(np.int32)[:, None].copy(), name="idxinitc"))
    bigpc = ap(nc.inline_tensor(
        (NEXP * CAP + np.arange(128, dtype=np.float32))[:, None].copy(), name="bigpc"))
    capendc = ap(nc.inline_tensor(
        np.broadcast_to((CAP * (np.arange(NEXP, dtype=np.float32) + 1))[None, :], (128, NEXP)).copy(),
        name="capendc"))

    P = Prog()
    SY, TE, VE, SC, GP = (P.engs[e] for e in ENGINES)
    sm = P._sem
    s_w, s_kv, s_at, s_ct, s_o, s_ms = (
        sm("s_w"), sm("s_kv"), sm("s_at"), sm("s_ct"), sm("s_o"), sm("s_ms"))
    s_x01 = [sm("s_x0"), sm("s_x1")]
    s_ev = [sm("s_ev0"), sm("s_ev1")]
    s_e2 = [sm("s_e20"), sm("s_e21")]
    s_h = [sm("s_h0"), sm("s_h1")]
    s_mk = sm("s_mk")
    s_h2 = sm("s_h2")
    s_tr = sm("s_tr")
    s_ic = sm("s_ic")
    s_sc01 = [sm("s_sc0"), sm("s_sc1")]
    s_idx = sm("s_idx")
    s_g = [sm("s_g0"), sm("s_g1")]
    s_gc = [sm("s_gc0"), sm("s_gc1")]
    s_wd = [sm("s_wd0"), sm("s_wd1")]
    s_cl = [sm("s_cl0"), sm("s_cl1")]
    s_cs = [sm("s_cs0"), sm("s_cs1")]
    s_sa = [sm(f"s_sa{h_}{p_}{ep_}") for h_ in range(4) for p_ in range(2)
            for ep_ in range(2)]
    pe = sm("pe")
    ve = sm("chain_vector")
    ac = sm("chain_scalar")
    gp = sm("chain_gpsimd")

    es = contextlib.ExitStack()

    def sb(stack, name, shape, dtype):
        return stack.enter_context(nc.sbuf_tensor(name, shape, dtype))

    with es:
        ident = sb(es, "ident", [128, 128], BF16)
        ones_mat = sb(es, "ones_mat", [128, 128], BF16)
        ones_col = sb(es, "ones_col", [128, 1], BF16)
        ones_rowb = sb(es, "ones_rowb", [1, 128], BF16)
        ones_rowf = sb(es, "ones_rowf", [1, 128], F32)
        bq_sb = sb(es, "bq_sb", [128, 16], F32)
        bk_sb = sb(es, "bk_sb", [128, 16], F32)
        bqs_sb = sb(es, "bqs_sb", [128, 16], F32)
        bv_row = sb(es, "bv_row", [1, D], BF16)
        wr_sb = sb(es, "wr_sb", [128, 16 * NEXP], BF16)
        Wbuf = sb(es, "Wbuf", [128, 16 * D], BF16)
        evb = sb(es, "evb", [128, 2 * 512], BF16)
        gates = sb(es, "gates", [128, 16 * NEXP], F32)
        colA = sb(es, "colA", [128, 1], F32)
        colB = sb(es, "colB", [128, 1], F32)
        colC = sb(es, "colC", [128, 1], F32)
        colD = sb(es, "colD", [128, 1], F32)
        scol = sb(es, "scol", [128, 1], F32)

        ps_a = es.enter_context(nc.psum_tensor("ps_a", [128, 512], F32))
        ps_b = es.enter_context(nc.psum_tensor("ps_b", [128, 512], F32))
        ps_c = es.enter_context(nc.psum_tensor("ps_c", [128, 512], F32))
        ps_d = es.enter_context(nc.psum_tensor("ps_d", [128, 512], F32))
        ps_e = es.enter_context(nc.psum_tensor("ps_e", [128, 512], F32))
        ps_f = es.enter_context(nc.psum_tensor("ps_f", [128, 512], F32))
        ps_t1 = es.enter_context(nc.psum_tensor("ps_t1", [128, 1024], BF16))
        ps_t2 = es.enter_context(nc.psum_tensor("ps_t2", [128, 1024], BF16))

        SY.dma(ident[:], identc[:], s_ms)
        VE.op("memset", ones_mat[:], 1.0)
        VE.op("memset", ones_col[:], 1.0)
        VE.op("memset", ones_rowb[:], 1.0)
        VE.op("memset", ones_rowf[:], 1.0)
        for c in range(16):
            SY.dma(bq_sb[:, c:c + 1], bq[c * 128:(c + 1) * 128, :], s_ms)
            SY.dma(bk_sb[:, c:c + 1], bk[c * 128:(c + 1) * 128, :], s_ms)
            GP.dma(wr_sb[:, c * NEXP:(c + 1) * NEXP], Wr[c * 128:(c + 1) * 128, :], s_w)
        GP.dma(bv_row[:], bv[:], s_w)
        VE.wait(s_ms)
        VE.wait(s_w)
        VE.op("tensor_scalar_mul", bqs_sb[:], bq_sb[:], SCALE)
        VE.op("memset", scol[:], SCALE)

        if "p1" in phases:
            p1 = contextlib.ExitStack()
            with p1:
                Wbuf2 = sb(p1, "Wbuf2", [128, 16 * D], BF16)
                evb2 = sb(p1, "evb2", [128, 2 * 512], BF16)
                sqb = sb(p1, "sqb", [128, D // 2], F32)
                xg = sb(p1, "xg", [128, 2 * D], F32)
                lnt = sb(p1, "lnt", [128, 2 * D], BF16)
                lnf2 = [sb(p1, "lnf0", [128, 16 * 512], BF16),
                        sb(p1, "lnf1", [128, 16 * 512], BF16)]

                for c in range(16):
                    GP.dma(Wbuf[:, c * D:(c + 1) * D], WkT[c * 128:(c + 1) * 128, :], s_w)
                    GP.dma(Wbuf2[:, c * D:(c + 1) * D], WvT[c * 128:(c + 1) * 128, :], s_w)
                w_ready_kv = s_w.n

                ev_slot_dma = [0, 0]
                ev2_slot_dma = [0, 0]
                xg_slot_user = [0, 0]
                xg_slot_ac = [0, 0]

                def ln_tile_to_fm(src_dram, row0, tile_idx4, xslot):
                    SY.wait(ve, xg_slot_user[xslot])
                    SY.wait(ac, xg_slot_ac[xslot])
                    sx = s_x01[xslot]
                    SY.dma(xg[:, xslot * D:(xslot + 1) * D], src_dram[row0:row0 + 128, :], sx)
                    xv = sx.n
                    xt_ = xg[:, xslot * D:(xslot + 1) * D]
                    VE.wait(sx, xv)
                    VE.op("reduce_sum", colA[:], xt_, axis=AX.X)
                    SC.wait(sx, xv)
                    SC.wait(ve)
                    SC.op("activation", sqb[:], xt_[:, 0:D // 2], AF.Square)
                    VE.wait(ac)
                    VE.op("reduce_sum", colB[:], sqb[:], axis=AX.X)
                    SC.wait(ve)
                    SC.op("activation", sqb[:], xt_[:, D // 2:D], AF.Square)
                    VE.wait(ac)
                    VE.op("reduce_sum", colC[:], sqb[:], axis=AX.X)
                    VE.op("tensor_add", colB[:], colB[:], colC[:])
                    VE.op("tensor_scalar_mul", colA[:], colA[:], 1.0 / D)
                    VE.op("tensor_scalar_mul", colB[:], colB[:], 1.0 / D)
                    VE.op("tensor_mul", colC[:], colA[:], colA[:])
                    VE.op("tensor_scalar", colB[:], colB[:], colC[:, 0:1], EPS,
                          ALU.subtract, ALU.add)
                    SC.wait(ve)
                    SC.op("activation", colC[:], colB[:], AF.Ln)
                    SC.op("activation", colD[:], colC[:], AF.Exp, scale=-0.5)
                    VE.wait(ac)
                    lslot = tile_idx4 % 2
                    VE.wait(pe, lnt_tp[lslot])
                    VE.op("tensor_scalar", lnt[:, lslot * D:(lslot + 1) * D], xt_,
                          colA[:, 0:1], colD[:, 0:1], ALU.subtract, ALU.mult)
                    xg_slot_user[xslot] = ve.n
                    xg_slot_ac[xslot] = ac.n
                    lnf = lnf2[gpar[0]]
                    ts_done = ve.n
                    TE.wait(ve, ts_done)
                    TE.wait(pe, lnf_used[gpar[0]])
                    for d in range(16):
                        tb = [ps_t1, ps_t2][d % 2]
                        TE.wait(ac, tp_ac[d % 2])
                        TE.op("transpose", tb[:, 0:128],
                              lnt[:, lslot * D + d * 128: lslot * D + (d + 1) * 128],
                              ident[:], inc=(pe, 1))
                        tp_pe[d % 2] = pe.n
                        SC.wait(pe, tp_pe[d % 2])
                        SC.op("activation",
                              lnf[:, d * 512 + tile_idx4 * 128: d * 512 + (tile_idx4 + 1) * 128],
                              tb[:, 0:128], AF.Identity)
                        tp_ac[d % 2] = ac.n
                    lnt_tp[lslot] = pe.n

                lnt_tp = [0, 0]
                tp_ac = [0, 0]
                tp_pe = [0, 0]
                gpar = [0]
                lnf_used = [0, 0]
                grp_tp = [0, 0]

                def group_fm(src_dram, g0):
                    for i in range(4):
                        ln_tile_to_fm(src_dram, g0 + i * 128, i, i % 2)
                    grp_tp[gpar[0]] = ac.n

                def proj_fm(Wb, bias_col, scale, dst, g, bank_sel, nchunks=16):
                    lnf = lnf2[gpar[0]]
                    for e in range(nchunks):
                        bank = bank_sel[e % 2]
                        TE.wait(ac, grp_tp[gpar[0]])
                        TE.wait(ac, bank["ac"])
                        for d in range(16):
                            TE.op("matmul", bank["ps"][:],
                                  Wb[:, d * D + e * 128: d * D + (e + 1) * 128],
                                  lnf[:, d * 512:(d + 1) * 512],
                                  start=(d == 0), stop=(d == 15),
                                  inc=(pe, 1) if d == 15 else None)
                        bank["pe"] = pe.n
                        slot = e % 2
                        SC.wait(pe, bank["pe"])
                        SC.wait(s_ev[slot], ev_slot_dma[slot])
                        SC.op("activation", evb[:, slot * 512:(slot + 1) * 512],
                              bank["ps"][:], AF.Identity,
                              bias=bias_col(e), scale=scale)
                        bank["ac"] = ac.n
                        SY.wait(ac, bank["ac"])
                        ev_slot_dma[slot] = SY.dma(dst(e, g), evb[:, slot * 512:(slot + 1) * 512], s_ev[slot])

                bankA = {"ps": ps_a, "pe": 0, "ac": 0}
                bankB = {"ps": ps_b, "pe": 0, "ac": 0}
                bankC = {"ps": ps_c, "pe": 0, "ve": 0}
                bankD = {"ps": ps_d, "pe": 0, "ve": 0}

                for g in range(16):
                    group_fm(xkv, g * 512)
                    if g == 0:
                        TE.wait(s_w, w_ready_kv)
                    lnf = lnf2[gpar[0]]
                    proj_fm(Wbuf, lambda e: bk_sb[:, e:e + 1], 1.0,
                            lambda e, gg=g: Kscr[e, :, gg * 512:(gg + 1) * 512],
                            g, [bankA, bankB])
                    vsel = [bankC, bankD]
                    vidx = 0
                    for eg in range(4):
                        for i in range(4):
                            bank = vsel[vidx % 2]
                            slot = vidx % 2
                            vidx += 1
                            TE.wait(ve, bank["ve"])
                            for d in range(16):
                                TE.op("matmul", bank["ps"][:],
                                      lnf[:, d * 512 + i * 128: d * 512 + (i + 1) * 128],
                                      Wbuf2[:, d * D + eg * 512: d * D + (eg + 1) * 512],
                                      start=(d == 0), stop=False)
                            TE.op("matmul", bank["ps"][:], ones_rowb[:],
                                  bv_row[:, eg * 512:(eg + 1) * 512],
                                  start=False, stop=True, inc=(pe, 1))
                            bank["pe"] = pe.n
                            VE.wait(pe, bank["pe"])
                            VE.wait(s_e2[slot], ev2_slot_dma[slot])
                            VE.op("tensor_copy", evb2[:, slot * 512:(slot + 1) * 512], bank["ps"][:])
                            bank["ve"] = ve.n
                            SY.wait(ve, bank["ve"])
                            ev2_slot_dma[slot] = SY.dma(
                                Vscr[eg * 4:(eg + 1) * 4, :, g * 4 + i, :].rearrange("hh p j -> p hh j"),
                                evb2[:, slot * 512:(slot + 1) * 512].rearrange("p (hh j) -> p hh j", j=128),
                                s_e2[slot])
                    lnf_used[gpar[0]] = pe.n
                    gpar[0] ^= 1

                pe_b4 = pe.n
                for c in range(16):
                    GP.wait(pe, pe_b4)
                    GP.dma(Wbuf[:, c * D:(c + 1) * D], WqT[c * 128:(c + 1) * 128, :], s_w)
                w_ready_q = s_w.n
                for g in range(4):
                    group_fm(xq, g * 512)
                    if g == 0:
                        TE.wait(s_w, w_ready_q)
                    proj_fm(Wbuf, lambda e: bqs_sb[:, e:e + 1], scol[:, 0:1],
                            lambda e, gg=g: Qscr[e, :, gg * 512:(gg + 1) * 512],
                            g, [bankA, bankB])
                    lnf_used[gpar[0]] = pe.n
                    gpar[0] ^= 1

        if "p2" in phases:
            p2 = contextlib.ExitStack()
            with p2:
                Kh = [sb(p2, "Kh0", [128, S], BF16), sb(p2, "Kh1", [128, S], BF16)]
                Vh = [sb(p2, "Vh0", [128, S], BF16), sb(p2, "Vh1", [128, S], BF16)]
                Qh = [sb(p2, "Qh0", [128, T], BF16), sb(p2, "Qh1", [128, T], BF16)]
                msk = sb(p2, "msk", [128, 16 * QC], BF16)
                sqs = sb(p2, "sqs", [128, 2 * 2048], BF16)
                cstg = sb(p2, "cstg", [128, 2 * 2048], BF16)
                pt = sb(p2, "pt", [128, 4 * QC], BF16)
                ctxh = sb(p2, "ctxh", [128, T], BF16)
                den_row = sb(p2, "den_row", [1, 2 * QC], F32)
                rec_row = sb(p2, "rec_row", [1, 2 * QC], F32)
                recb = sb(p2, "recb", [128, QC], F32)
                cxe = sb(p2, "cxe", [128, 2 * QC], F32)
                negc = sb(p2, "negc", [128, 1], F32)
                kq2 = sb(p2, "kq2", [128, 1], F32)

                SY.wait(s_ev[0])
                SY.wait(s_ev[1])
                SY.wait(s_e2[0])
                SY.wait(s_e2[1])
                head_mm_done = [0, 0]
                ld_done = [0, 0]
                SY.dma(msk[:].rearrange("p (j q) -> p j q", q=QC),
                       masks.rearrange("j p q -> p j q"), s_mk)
                mv = s_mk.n
                SY.dma(Kh[0][:], Kscr[0], s_h[0])
                SY.dma(Vh[0][:].rearrange("p (t j) -> p t j", j=DK), Vscr[0], s_h[0])
                SY.dma(Qh[0][:], Qscr[0], s_h[0])
                ld_done[0] = s_h[0].n
                ctx_dma = [0]
                st2 = {"ctx_free": 0, "den_free": 0,
                       "recb_user": 0, "rec_user_pe": 0, "cxe_user": [0, 0]}
                pend = []

                cchunks = []
                for e_ in range(NEXP):
                    for src_, dst_ in ((Wg, WgB), (Wu, WuB), (Wd, WdB)):
                        for c_ in range(16):
                            cchunks.append((src_, dst_, e_, c_))
                cv = {"load_i": 0, "store_i": 0, "cl_v": [0, 0], "cs_g": [0, 0]}
                p1_quiesce = (pe.n, ac.n, ve.n)

                def emit_conv(n):
                    for _ in range(n):
                        j = cv["store_i"]
                        if j >= len(cchunks):
                            return
                        while cv["load_i"] <= min(j + 1, len(cchunks) - 1):
                            li = cv["load_i"]
                            slot = li % 2
                            src_, dst_, e_, c_ = cchunks[li]
                            if cv["load_i"] == 0:
                                GP.wait(pe, p1_quiesce[0])
                                GP.wait(ac, p1_quiesce[1])
                                GP.wait(ve, p1_quiesce[2])
                                GP.wait(s_ev[0])
                                GP.wait(s_ev[1])
                                GP.wait(s_e2[0])
                                GP.wait(s_e2[1])
                                GP.wait(s_x01[0])
                                GP.wait(s_x01[1])
                            GP.wait(s_cs[slot], cv["cs_g"][slot])
                            GP.dma(cstg[:, slot * 2048:(slot + 1) * 2048],
                                   src_[e_, c_ * 128:(c_ + 1) * 128, :], s_cl[slot])
                            cv["cl_v"][slot] = s_cl[slot].n
                            cv["load_i"] += 1
                        slot = j % 2
                        src_, dst_, e_, c_ = cchunks[j]
                        GP.wait(s_cl[slot], cv["cl_v"][slot])
                        GP.dma(dst_[e_, c_ * 128:(c_ + 1) * 128, :],
                               cstg[:, slot * 2048:(slot + 1) * 2048], s_cs[slot])
                        cv["cs_g"][slot] = s_cs[slot].n
                        cv["store_i"] += 1

                for h in range(H):
                    sl = h % 2
                    Khc, Vhc, Qhc = Kh[sl], Vh[sl], Qh[sl]
                    emit_conv(28)
                    at_ready = ld_done[sl]
                    if h + 1 < H:
                        nsl = (h + 1) % 2
                        SY.wait(pe, head_mm_done[nsl])
                        SY.dma(Kh[nsl][:], Kscr[h + 1], s_h[nsl])
                        SY.dma(Vh[nsl][:].rearrange("p (t j) -> p t j", j=DK), Vscr[h + 1], s_h[nsl])
                        SY.dma(Qh[nsl][:], Qscr[h + 1], s_h[nsl])
                        ld_done[nsl] = s_h[nsl].n

                    SC.wait(s_h[sl], at_ready)
                    TE.wait(ac)
                    TE.wait(ve)
                    nrm_ve = [0, 0]
                    sq_pe = [0, 0]
                    sqi = 0
                    for which, buf, width in (("q", Qhc, T), ("k", Khc, S)):
                        first = True
                        for idx in range(width // 2048):
                            sslot = sqi % 2
                            sqi += 1
                            SC.wait(pe, sq_pe[sslot])
                            SC.op("activation", sqs[:, sslot * 2048:(sslot + 1) * 2048],
                                  buf[:, idx * 2048:(idx + 1) * 2048], AF.Square)
                            sq_ac = ac.n
                            for q4 in range(4):
                                bsel = q4 % 2
                                bank = [ps_a, ps_b][bsel]
                                TE.wait(ac, sq_ac)
                                TE.wait(ve, nrm_ve[bsel])
                                TE.op("matmul", bank[:], ones_mat[:],
                                      sqs[:, sslot * 2048 + q4 * 512: sslot * 2048 + (q4 + 1) * 512],
                                      start=True, stop=True, inc=(pe, 1))
                                mmv = pe.n
                                VE.wait(pe, mmv)
                                VE.op("reduce_max", colA[:], bank[:], axis=AX.X)
                                if first:
                                    VE.op("tensor_copy", colB[:], colA[:])
                                    first = False
                                else:
                                    VE.op("tensor_max", colB[:], colB[:], colA[:])
                                nrm_ve[bsel] = ve.n
                            sq_pe[sslot] = pe.n
                        if which == "q":
                            VE.op("tensor_copy", colC[:], colB[:])
                    VE.op("tensor_mul", kq2[:], colC[:], colB[:])
                    SC.wait(ve)
                    SC.op("activation", colD[:], kq2[:], AF.Ln)
                    SC.op("activation", negc[:], colD[:], AF.Exp, scale=0.5)
                    VE.wait(ac)
                    VE.op("tensor_scalar_mul", negc[:], negc[:], float(-1.02 * SCALE))
                    negc_ve = ve.n
                    TE.wait(ve, negc_ve)

                    def finish_norm(pd):
                        k0, slot, rec_ready = pd
                        TE.wait(ve, rec_ready)
                        TE.wait(ve, st2["recb_user"])
                        TE.op("matmul", ps_e[:], ones_rowf[:], rec_row[:, slot * QC:(slot + 1) * QC],
                              start=True, stop=True, inc=(pe, 1))
                        st2["rec_user_pe"] = pe.n
                        VE.wait(pe, st2["rec_user_pe"])
                        VE.op("tensor_copy", recb[:], ps_e[:])
                        VE.wait(s_ct, ctx_dma[0])
                        VE.op("tensor_mul", ctxh[:, k0 * QC:(k0 + 1) * QC],
                              cxe[:, slot * QC:(slot + 1) * QC], recb[:])
                        st2["recb_user"] = ve.n
                        st2["cxe_user"][slot] = ve.n

                    for k in range(NCHUNK):
                        ntile = 16 * (k + 1)
                        sbank = [{"ps": ps_a, "ac": 0}, {"ps": ps_b, "ac": 0}, {"ps": ps_f, "ac": 0}]
                        pt_free = [0, 0, 0, 0]
                        exp_ac = [0] * ntile
                        mask_ve = [0] * ntile
                        sj_pe = [0] * ntile
                        for step in range(ntile + 1):
                            if step < ntile:
                                j = step
                                bk_ = sbank[j % 3]
                                TE.wait(ac, bk_["ac"])
                                TE.op("matmul", bk_["ps"][:],
                                      Khc[:, j * 128:(j + 1) * 128],
                                      Qhc[:, k * QC:(k + 1) * QC],
                                      start=True, stop=True, inc=(pe, 1))
                                sj_pe[j] = pe.n
                                ptt = pt[:, (j % 4) * QC:(j % 4 + 1) * QC]
                                SC.wait(pe, max(sj_pe[j], pt_free[j % 4]))
                                SC.wait(ve, negc_ve)
                                SC.op("activation", ptt, bk_["ps"][:], AF.Exp, bias=negc[:, 0:1])
                                exp_ac[j] = ac.n
                                bk_["ac"] = ac.n
                                if j >= ntile - 16:
                                    m = j - (ntile - 16)
                                    VE.wait(s_mk, mv)
                                    VE.wait(ac, exp_ac[j])
                                    VE.op("tensor_mul", ptt, ptt, msk[:, m * QC:(m + 1) * QC])
                                    mask_ve[j] = ve.n
                            if step == 4 and len(pend) == 2:
                                finish_norm(pend.pop(0))
                            if step >= 1:
                                i = step - 1
                                ptt = pt[:, (i % 4) * QC:(i % 4 + 1) * QC]
                                TE.wait(ac, exp_ac[i])
                                if mask_ve[i]:
                                    TE.wait(ve, mask_ve[i])
                                if i == 0:
                                    TE.wait(ve, st2["ctx_free"])
                                    TE.wait(ve, st2["den_free"])
                                TE.op("matmul", ps_d[0:1, :], ones_col[:], ptt,
                                      start=(i == 0), stop=(i == ntile - 1))
                                TE.op("matmul", ps_c[:], Vhc[:, i * 128:(i + 1) * 128], ptt,
                                      start=(i == 0), stop=(i == ntile - 1), inc=(pe, 1))
                                pt_free[i % 4] = pe.n
                        head_mm_done[sl] = pe.n
                        slot = k % 2
                        VE.wait(pe)
                        VE.op("tensor_copy", cxe[:, slot * QC:(slot + 1) * QC], ps_c[:])
                        st2["ctx_free"] = ve.n
                        VE.op("tensor_copy", den_row[:, slot * QC:(slot + 1) * QC], ps_d[0:1, :])
                        st2["den_free"] = ve.n
                        VE.wait(pe, st2["rec_user_pe"])
                        VE.op("reciprocal", rec_row[:, slot * QC:(slot + 1) * QC],
                              den_row[:, slot * QC:(slot + 1) * QC])
                        pend.append((k, slot, ve.n))
                    while pend:
                        finish_norm(pend.pop(0))
                    SY.wait(ve)
                    ctx_dma[0] = SY.dma(CTX[h], ctxh[:], s_ct)

        if "p3" in phases:
            p3 = contextlib.ExitStack()
            with p3:
                ctxfull = sb(p3, "ctxfull", [128, H * T], BF16)
                bo_row = sb(p3, "bo_row", [1, D], BF16)
                sqb = sb(p3, "sqb3", [128, D // 2], F32)
                xqt = sb(p3, "xqt", [128, 2 * D], F32)
                x2t = sb(p3, "x2t", [128, D], F32)
                h2t = sb(p3, "h2t", [128, D], BF16)
                trb = sb(p3, "trb", [128, D], BF16)
                logit = sb(p3, "logit", [128, NEXP], F32)
                tcC = sb(p3, "tcC", [128, NEXP], F32)
                tcD = sb(p3, "tcD", [128, NEXP], F32)
                tcE = sb(p3, "tcE", [128, NEXP], F32)
                if MOE == "sparse":
                    lt_sb = sb(p3, "lt_sb", [128, 128], BF16)
                    tokid_sb = sb(p3, "tokid_sb", [128, 16], mybir.dt.int32)
                    offs_sb = sb(p3, "offs_sb", [1, NEXP], F32)
                    capend_sb = sb(p3, "capend_sb", [128, NEXP], F32)
                    selx = sb(p3, "selx", [128, NEXP], F32)
                    selb = sb(p3, "selb", [128, NEXP], BF16)
                    pfx = sb(p3, "pfx", [128, NEXP], F32)
                    slotf = sb(p3, "slotf", [128, NEXP], F32)
                    slot_i = sb(p3, "slot_i", [128, 2 * NEXP], mybir.dt.int32)
                    bigp_sb = sb(p3, "bigp_sb", [128, 1], F32)
                    totrow = sb(p3, "totrow", [1, NEXP], F32)
                    baserow = sb(p3, "baserow", [1, NEXP], F32)
                    boff = sb(p3, "boff", [1, NEXP], F32)

                pe_b4 = pe.n
                GP.wait(pe, pe_b4)
                GP.dma(bo_row[:], bo[:], s_w)
                for c in range(16):
                    GP.wait(pe, pe_b4)
                    GP.dma(Wbuf[:, c * D:(c + 1) * D], WoT[c * 128:(c + 1) * 128, :], s_w)
                w_ready_o = s_w.n
                trash_kv0 = [0]
                if MOE == "sparse":
                    SY.dma(lt_sb[:], ltc[:], s_ms)
                    SY.dma(tokid_sb[:], tokidc[:], s_ms)
                    SY.dma(offs_sb[:], offsc[:], s_ms)
                    SY.dma(capend_sb[:], capendc[:], s_ms)
                    SY.dma(bigp_sb[:], bigpc[:], s_ms)
                    VE.op("memset", baserow[:], 0.0)
                    VE.wait(s_ms)
                    gp_init = GP.dma(IDXTOK[:], idxinitc[:], s_ic)
                    VE.op("memset", tcC[:], 0.0)
                    VE.op("memset", h2t[:], 0.0)
                    trash_ve = ve.n
                    GP.wait(ve, trash_ve)
                    GP.dma(H2TM[T:T + 128, :], h2t[:], s_tr)
                    GP.dma(GATES_D[T:T + 128, :], tcC[:], s_tr)
                    trash_kv0[0] = s_tr.n
                    GP.wait(s_ic, gp_init)
                SY.wait(s_ct)
                for hh in range(H):
                    SY.dma(ctxfull[:, hh * T:(hh + 1) * T], CTX[hh], s_at)
                ctx_ready = s_at.n
                xq_dma = [0, 0]
                xq_user = [0, 0]
                SY.dma(xqt[:, 0:D], xq[0:128, :], s_x01[0])
                xq_dma[0] = s_x01[0].n
                ob = [{"ps": ps_a, "ve": 0}, {"ps": ps_b, "ve": 0}]
                oidx = 0
                x2_dma = [0]
                h2_tp = [0]
                tr_dma = [0]
                x2_sc = [0]
                rtr_pe = [0]
                tp_ac3 = [0, 0]
                tp_pe3 = [0, 0]
                h2tm_dma = [0]
                idx_ve = [0]
                slot_sc = [0, 0]

                for i in range(16):
                    if i + 1 < 16:
                        SY.wait(ve, xq_user[(i + 1) % 2])
                        SY.dma(xqt[:, ((i + 1) % 2) * D:((i + 1) % 2 + 1) * D],
                               xq[(i + 1) * 128:(i + 2) * 128, :], s_x01[(i + 1) % 2])
                        xq_dma[(i + 1) % 2] = s_x01[(i + 1) % 2].n
                    if i == 0:
                        TE.wait(s_w, w_ready_o)
                        TE.wait(s_at, ctx_ready)
                    for eg in range(4):
                        bk_ = ob[oidx % 2]
                        oidx += 1
                        TE.wait(ve, bk_["ve"])
                        for cd in range(16):
                            TE.op("matmul", bk_["ps"][:],
                                  ctxfull[:, cd * T + i * 128: cd * T + (i + 1) * 128],
                                  Wbuf[:, cd * D + eg * 512: cd * D + (eg + 1) * 512],
                                  start=(cd == 0), stop=False)
                        TE.op("matmul", bk_["ps"][:], ones_rowb[:],
                              bo_row[:, eg * 512:(eg + 1) * 512],
                              start=False, stop=True, inc=(pe, 1))
                        VE.wait(pe)
                        VE.wait(s_x01[i % 2], xq_dma[i % 2])
                        VE.wait(s_o, x2_dma[0])
                        VE.wait(ac, x2_sc[0])
                        VE.op("tensor_add", x2t[:, eg * 512:(eg + 1) * 512], bk_["ps"][:],
                              xqt[:, (i % 2) * D + eg * 512:(i % 2) * D + (eg + 1) * 512])
                        bk_["ve"] = ve.n
                    xq_user[i % 2] = ve.n
                    SY.wait(ve)
                    x2_dma[0] = SY.dma(out[i * 128:(i + 1) * 128, :], x2t[:], s_o)
                    VE.op("reduce_sum", colA[:], x2t[:], axis=AX.X)
                    SC.wait(ve)
                    SC.op("activation", sqb[:], x2t[:, 0:D // 2], AF.Square)
                    VE.wait(ac)
                    VE.op("reduce_sum", colB[:], sqb[:], axis=AX.X)
                    SC.wait(ve)
                    SC.op("activation", sqb[:], x2t[:, D // 2:D], AF.Square)
                    VE.wait(ac)
                    VE.op("reduce_sum", colC[:], sqb[:], axis=AX.X)
                    VE.op("tensor_add", colB[:], colB[:], colC[:])
                    VE.op("tensor_scalar_mul", colA[:], colA[:], 1.0 / D)
                    VE.op("tensor_scalar_mul", colB[:], colB[:], 1.0 / D)
                    VE.op("tensor_mul", colC[:], colA[:], colA[:])
                    VE.op("tensor_scalar", colB[:], colB[:], colC[:, 0:1], EPS,
                          ALU.subtract, ALU.add)
                    SC.wait(ve)
                    SC.op("activation", colC[:], colB[:], AF.Ln)
                    SC.op("activation", colD[:], colC[:], AF.Exp, scale=-0.5)
                    x2_sc[0] = ac.n
                    VE.wait(ac)
                    VE.wait(pe, h2_tp[0])
                    VE.wait(s_h2, h2tm_dma[0])
                    VE.wait(s_tr, trash_kv0[0])
                    VE.op("tensor_scalar", h2t[:], x2t[:], colA[:, 0:1], colD[:, 0:1],
                          ALU.subtract, ALU.mult)
                    h2_ve = ve.n
                    TE.wait(ve, h2_ve)
                    TE.wait(pe, rtr_pe[0])
                    for d in range(16):
                        tb = [ps_t1, ps_t2][d % 2]
                        TE.wait(ac, tp_ac3[d % 2])
                        TE.op("transpose", tb[:, 0:128], h2t[:, d * 128:(d + 1) * 128],
                              ident[:], inc=(pe, 1))
                        tp_pe3[d % 2] = pe.n
                        SC.wait(pe, tp_pe3[d % 2])
                        SC.op("activation", trb[:, d * 128:(d + 1) * 128],
                              tb[:, 0:128], AF.Identity)
                        tp_ac3[d % 2] = ac.n
                    h2_tp[0] = pe.n
                    trb_ac = ac.n
                    TE.wait(ac, trb_ac)
                    TE.wait(ve)
                    for d in range(16):
                        TE.op("matmul", ps_f[:, 0:NEXP],
                              trb[:, d * 128:(d + 1) * 128],
                              wr_sb[:, d * NEXP:(d + 1) * NEXP],
                              start=(d == 0), stop=(d == 15),
                              inc=(pe, 1) if d == 15 else None)
                    rtr_pe[0] = pe.n
                    VE.wait(pe)
                    VE.op("tensor_copy", logit[:], ps_f[:, 0:NEXP])
                    VE.op("reduce_max", colA[:], logit[:], axis=AX.X)
                    VE.wait(s_tr, trash_kv0[0])
                    VE.op("tensor_single_scalar", tcC[:], logit[:], colA[:, 0:1], ALU.is_equal)
                    VE.op("scalar_tensor_tensor", tcD[:], tcC[:], -1e30, logit[:], ALU.mult, ALU.add)
                    VE.op("reduce_max", colB[:], tcD[:], axis=AX.X)
                    VE.op("tensor_sub", colC[:], colA[:], colB[:])
                    SC.wait(ve)
                    SC.op("activation", colD[:], colC[:], AF.Sigmoid)
                    VE.wait(ac)
                    VE.op("tensor_scalar", colC[:], colD[:], -1.0, 1.0, ALU.mult, ALU.add)
                    VE.op("tensor_single_scalar", tcD[:], logit[:], colB[:, 0:1], ALU.is_equal)
                    VE.op("tensor_scalar", tcE[:], tcC[:], -1.0, 1.0, ALU.mult, ALU.add)
                    VE.op("tensor_mul", tcD[:], tcD[:], tcE[:])
                    VE.op("tensor_scalar_mul", tcC[:], tcC[:], colD[:, 0:1])
                    VE.op("tensor_scalar_mul", tcD[:], tcD[:], colC[:, 0:1])
                    VE.op("tensor_add", gates[:, i * NEXP:(i + 1) * NEXP], tcC[:], tcD[:])
                    if MOE == "sparse":
                        gts = gates[:, i * NEXP:(i + 1) * NEXP]
                        gates_ve = ve.n
                        SY.wait(ve, gates_ve)
                        SY.dma(GATES_D[i * 128:(i + 1) * 128, :], gts, s_kv)
                        SY.wait(ve, h2_ve)
                        h2tm_dma[0] = SY.dma(H2TM[i * 128:(i + 1) * 128, :], h2t[:], s_h2)
                        VE.op("tensor_single_scalar", selx[:], gts, 0.0, ALU.is_gt)
                        VE.op("tensor_copy", selb[:], selx[:])
                        VE.op("tensor_add", boff[:], baserow[:], offs_sb[:])
                        sel_ve = ve.n
                        TE.wait(ve, sel_ve)
                        TE.wait(ve, idx_ve[0])
                        TE.op("matmul", ps_d[:, 0:NEXP], lt_sb[:], selb[:],
                              start=True, stop=True)
                        TE.op("matmul", ps_c[0:1, 0:NEXP], ones_col[:], selb[:],
                              start=True, stop=True)
                        TE.op("matmul", ps_e[:, 0:NEXP], ones_rowf[:], boff[:],
                              start=True, stop=True, inc=(pe, 1))
                        idx_pe = pe.n
                        spar = i % 2
                        VE.wait(pe, idx_pe)
                        VE.wait(s_sc01[spar], slot_sc[spar])
                        VE.op("tensor_copy", pfx[:], ps_d[:, 0:NEXP])
                        VE.op("tensor_add", pfx[:], pfx[:], ps_e[:, 0:NEXP])
                        VE.op("tensor_tensor", slotf[:], pfx[:], capend_sb[:], ALU.is_lt)
                        VE.op("tensor_mul", slotf[:], slotf[:], selx[:])
                        VE.op("tensor_scalar", pfx[:], pfx[:], bigp_sb[:, 0:1], None, ALU.subtract)
                        VE.op("tensor_mul", pfx[:], pfx[:], slotf[:])
                        VE.op("tensor_scalar", pfx[:], pfx[:], bigp_sb[:, 0:1], None, ALU.add)
                        VE.op("tensor_copy", slot_i[:, spar * NEXP:(spar + 1) * NEXP], pfx[:])
                        VE.op("tensor_copy", totrow[:], ps_c[0:1, 0:NEXP])
                        VE.op("tensor_add", baserow[:], baserow[:], totrow[:])
                        idx_ve[0] = ve.n
                        for e in range(NEXP):
                            GP.wait(ve, idx_ve[0])
                            GP.op("indirect_dma_start",
                                  out=IDXTOK[:], 
                                  out_offset=bass.IndirectOffsetOnAxis(
                                      ap=slot_i[:, spar * NEXP + e: spar * NEXP + e + 1], axis=0),
                                  in_=tokid_sb[:, i:i + 1],
                                  in_offset=None,
                                  inc=(s_sc01[spar], 16), chain=False)
                        slot_sc[spar] = s_sc01[spar].n

        if "p4" in phases and MOE == "sparse":
            NSG = CAP // 512
            NCT = CAP // 128
            p4 = contextlib.ExitStack()
            with p4:
                gfm = sb(p4, "gfm", [128, 16 * CAP], BF16)
                gusp = sb(p4, "gusp", [128, 16 * CAP], BF16)
                wdq = [sb(p4, "wdq0", [128, 16 * 512], BF16),
                       sb(p4, "wdq1", [128, 16 * 512], BF16)]
                gbuf = sb(p4, "gbuf", [128, 2 * D], BF16)
                hsc = sb(p4, "hsc", [128, 2 * 512], F32)
                idxg2 = [sb(p4, "idxg0", [128, NCT], mybir.dt.int32),
                         sb(p4, "idxg1", [128, NCT], mybir.dt.int32)]
                gcol2 = [sb(p4, "gcol0", [128, NCT], F32),
                         sb(p4, "gcol1", [128, NCT], F32)]

                gsel = [{"ps": ps_a, "ev": (ve, 0)}, {"ps": ps_b, "ev": (ve, 0)}]
                hsc_guard = [None, None]
                wdq_user = [0, 0]
                wq_rdy = [0, 0]
                first_accum = [True]
                gbuf_tp = [0, 0]
                tp_ac4 = [0, 0]
                tp_pe4 = [0, 0]
                gcol_user2 = [0, 0]

                SY.wait(s_sc01[0])
                SY.wait(s_sc01[1])
                SY.wait(s_kv)
                SY.wait(s_h2)
                SY.wait(s_tr)
                pe_b4 = pe.n
                GP.wait(s_cs[0])
                GP.wait(s_cs[1])
                for c in range(16):
                    GP.wait(pe, pe_b4)
                    GP.dma(Wbuf[:, c * D:(c + 1) * D], WgB[0, c * 128:(c + 1) * 128, :], s_w)
                wg_rdy = s_w.n
                TE.wait(ac)
                for e in range(NEXP):
                    es = e % 2
                    idxg = idxg2[es]
                    gcol = gcol2[es]
                    SY.wait(s_g[0])
                    SY.wait(s_g[1])
                    SY.wait(s_gc[0])
                    SY.wait(s_gc[1])
                    for h_ in range(4):
                        for p_ in range(2):
                            SY.wait(s_sa[h_ * 4 + p_ * 2 + es])
                    SY.wait(ve, gcol_user2[es])
                    for ct in range(NCT):
                        SY.dma(idxg[:, ct:ct + 1], IDXTOK[e * CAP + ct * 128: e * CAP + (ct + 1) * 128, :], s_idx)
                    idx_rdy = s_idx.n
                    for ct in range(NCT):
                        gsl = ct % 2
                        GP.wait(s_idx, idx_rdy)
                        GP.wait(pe, gbuf_tp[gsl])
                        GP.op("indirect_dma_start",
                              out=gbuf[:, gsl * D:(gsl + 1) * D],
                              out_offset=None,
                              in_=H2TM[:],
                              in_offset=bass.IndirectOffsetOnAxis(ap=idxg[:, ct:ct + 1], axis=0),
                              inc=(s_g[gsl], 16), chain=False)
                        GP.op("indirect_dma_start",
                              out=gcol[:, ct:ct + 1],
                              out_offset=None,
                              in_=GATES_D[:],
                              in_offset=bass.IndirectOffsetOnAxis(ap=idxg[:, ct:ct + 1], axis=0),
                              element_offset=e,
                              inc=(s_gc[ct % 2], 16), chain=False)
                        TE.wait(s_g[gsl], s_g[gsl].n)
                        TE.wait(s_gc[ct % 2], s_gc[ct % 2].n)
                        for d in range(16):
                            tb = [ps_t1, ps_t2][d % 2]
                            TE.wait(ac, tp_ac4[d % 2])
                            TE.op("transpose", tb[:, 0:128],
                                  gbuf[:, gsl * D + d * 128: gsl * D + (d + 1) * 128],
                                  ident[:], inc=(pe, 1))
                            tp_pe4[d % 2] = pe.n
                            SC.wait(pe, tp_pe4[d % 2])
                            SC.op("activation",
                                  gfm[:, d * CAP + ct * 128: d * CAP + (ct + 1) * 128],
                                  tb[:, 0:128], AF.Identity)
                            tp_ac4[d % 2] = ac.n
                        gbuf_tp[gsl] = pe.n
                    gfm_ac = ac.n
                    gidx = 0
                    for so, sw in SEGS:
                        for ic in range(16):
                            bk_ = gsel[gidx % 2]
                            gidx += 1
                            TE.wait(ac, gfm_ac)
                            if gidx == 1:
                                TE.wait(s_w, wg_rdy)
                            TE.wait(bk_["ev"][0], bk_["ev"][1])
                            for d in range(16):
                                TE.op("matmul", bk_["ps"][:, 0:sw],
                                      Wbuf[:, d * D + ic * 128: d * D + (ic + 1) * 128],
                                      gfm[:, d * CAP + so: d * CAP + so + sw],
                                      start=(d == 0), stop=(d == 15),
                                      inc=(pe, 1) if d == 15 else None)
                            mm_done = pe.n
                            SC.wait(pe, mm_done)
                            SC.op("activation", gusp[:, ic * CAP + so: ic * CAP + so + sw],
                                  bk_["ps"][:, 0:sw], AF.Silu)
                            bk_["ev"] = (ac, ac.n)
                    pe_b4u = pe.n
                    for c in range(16):
                        GP.wait(pe, pe_b4u)
                        GP.dma(Wbuf[:, c * D:(c + 1) * D], WuB[e, c * 128:(c + 1) * 128, :], s_w)
                    wu_rdy = s_w.n
                    gidx = 0
                    for so, sw in SEGS:
                        for ic in range(16):
                            bk_ = gsel[gidx % 2]
                            gidx += 1
                            if gidx == 1:
                                TE.wait(s_w, wu_rdy)
                            TE.wait(bk_["ev"][0], bk_["ev"][1])
                            for d in range(16):
                                TE.op("matmul", bk_["ps"][:, 0:sw],
                                      Wbuf[:, d * D + ic * 128: d * D + (ic + 1) * 128],
                                      gfm[:, d * CAP + so: d * CAP + so + sw],
                                      start=(d == 0), stop=(d == 15),
                                      inc=(pe, 1) if d == 15 else None)
                            mm_done = pe.n
                            VE.wait(pe, mm_done)
                            VE.op("scalar_tensor_tensor",
                                  gusp[:, ic * CAP + so: ic * CAP + so + sw],
                                  gusp[:, ic * CAP + so: ic * CAP + so + sw],
                                  7.0, bk_["ps"][:, 0:sw], ALU.min, ALU.mult)
                            bk_["ev"] = (ve, ve.n)
                    pe_after_u = pe.n
                    first_d = True
                    for hg in range(4):
                        qsl = hg % 2
                        for c in range(16):
                            GP.wait(pe, wdq_user[qsl])
                            GP.dma(wdq[qsl][:, c * 512:(c + 1) * 512],
                                   WdB[e, c * 128:(c + 1) * 128, hg * 512:(hg + 1) * 512],
                                   s_wd[qsl])
                        wq_rdy[qsl] = s_wd[qsl].n
                        if hg == 0 and e + 1 < NEXP:
                            for c in range(16):
                                GP.wait(pe, pe_after_u)
                                GP.dma(Wbuf[:, c * D:(c + 1) * D],
                                       WgB[e + 1, c * 128:(c + 1) * 128, :], s_w)
                            wg_rdy = s_w.n
                        for ct in range(NCT):
                            bk_ = gsel[ct % 2]
                            hslot = ct % 2
                            sas = s_sa[hg * 4 + (ct % 2) * 2 + es]
                            TE.wait(bk_["ev"][0], bk_["ev"][1])
                            TE.wait(s_wd[qsl], wq_rdy[qsl])
                            if first_d:
                                TE.wait(ve)
                                first_d = False
                            for ic in range(16):
                                TE.op("matmul", bk_["ps"][:],
                                      gusp[:, ic * CAP + ct * 128: ic * CAP + (ct + 1) * 128],
                                      wdq[qsl][:, ic * 512:(ic + 1) * 512],
                                      start=(ic == 0), stop=(ic == 15),
                                      inc=(pe, 1) if ic == 15 else None)
                            mm_done = pe.n
                            if ct >= NCT - 2:
                                wdq_user[qsl] = mm_done
                            VE.wait(pe, mm_done)
                            if hsc_guard[hslot] is not None:
                                VE.wait(hsc_guard[hslot][0], hsc_guard[hslot][1])
                            VE.op("tensor_scalar_mul", hsc[:, hslot * 512:(hslot + 1) * 512],
                                  bk_["ps"][:], gcol[:, ct:ct + 1])
                            bk_["ev"] = (ve, ve.n)
                            gcol_user2[es] = ve.n
                            GP.wait(ve, bk_["ev"][1])
                            if first_accum[0]:
                                GP.wait(s_o)
                                first_accum[0] = False
                            if ct < 2:
                                for p_ in range(2):
                                    for ep_ in range(2):
                                        GP.wait(s_sa[hg * 4 + p_ * 2 + ep_])
                            GP.op("indirect_dma_start",
                                  out=out[:],
                                  out_offset=bass.IndirectOffsetOnAxis(ap=idxg[:, ct:ct + 1], axis=0),
                                  in_=hsc[:, hslot * 512:(hslot + 1) * 512],
                                  in_offset=None,
                                  element_offset=hg * 512,
                                  compute_op=ALU.add,
                                  inc=(sas, 16), chain=False)
                            hsc_guard[hslot] = (sas, sas.n)

        for d, src in dbg_outs:
            SY.wait(s_kv)
            SY.wait(s_ct)
            SY.dma(d, src, s_o)
        for s_ in s_sa:
            SY.wait(s_)
        SY.wait(s_o)
        SY.wait(s_kv)
        SY.wait(s_h2)
        SY.wait(s_tr)
        SY.wait(s_ct)

        with contextlib.ExitStack() as ss:
            for s in P.sems.values():
                s.h = ss.enter_context(nc.semaphore(s.name))
            with nc.Block() as block:
                @block.sync
                def _(e):
                    P.engs["sync"].replay(e)

                @block.tensor
                def _(e):
                    P.engs["tensor"].replay(e)

                @block.vector
                def _(e):
                    P.engs["vector"].replay(e)

                @block.scalar
                def _(e):
                    P.engs["scalar"].replay(e)

                @block.gpsimd
                def _(e):
                    P.engs["gpsimd"].replay(e)

    return nc



def _build_masks(cp):
    m = np.zeros((NCHUNK, 16, 128, QC), dtype=np.float32)
    for k in range(NCHUNK):
        span = (k + 1) * 2048
        q0 = k * 2048 + cp * QC
        kv = (span - 2048) + np.arange(16 * 128).reshape(16, 128)
        q = q0 + np.arange(QC)
        m[k] = (kv[:, :, None] <= q[None, None, :]).astype(np.float32)
    return m.astype(BF)


_NC_CACHE = {}


def _get_nc(key=("p1", "p2", "p3", "p4"), dbg=()):
    ck = (key, dbg)
    if ck not in _NC_CACHE:
        _NC_CACHE[ck] = build_nc(phases=key, dbg=dbg)
    return _NC_CACHE[ck]


def make_in_maps(x, Wq, bq, Wk, bk, Wv, bv, Wo, bo, Wr, Wg, Wu, Wd):
    x = np.asarray(x, dtype=np.float32)
    shared = dict(
        WqT=np.ascontiguousarray(np.asarray(Wq, np.float32).T),
        WkT=np.ascontiguousarray(np.asarray(Wk, np.float32).T),
        WvT=np.ascontiguousarray(np.asarray(Wv, np.float32).T),
        WoT=np.ascontiguousarray(np.asarray(Wo, np.float32).T),
        bq=np.asarray(bq, np.float32), bk=np.asarray(bk, np.float32),
        bv=np.asarray(bv, np.float32), bo=np.asarray(bo, np.float32),
        Wr=np.ascontiguousarray(np.asarray(Wr, np.float32)),
        Wg=np.ascontiguousarray(np.asarray(Wg, np.float32)),
        Wu=np.ascontiguousarray(np.asarray(Wu, np.float32)),
        Wd=np.ascontiguousarray(np.asarray(Wd, np.float32)),
    )
    in_maps = []
    for c in range(NCORE):
        b, cp = c // 4, c % 4
        rows = [x[b, k * 2048 + cp * QC: k * 2048 + (cp + 1) * QC] for k in range(NCHUNK)]
        in_maps.append(dict(shared,
                            xq=np.ascontiguousarray(np.concatenate(rows, 0)),
                            xkv=np.ascontiguousarray(x[b]),
                            masks=np.ascontiguousarray(_build_masks(cp)[0])))
    return in_maps


def assemble_out(results):
    outf = np.empty((NB, S, D), dtype=np.float32)
    for c in range(NCORE):
        b, cp = c // 4, c % 4
        o = results[c]["out"]
        for k in range(NCHUNK):
            outf[b, k * 2048 + cp * QC: k * 2048 + (cp + 1) * QC] = o[k * QC:(k + 1) * QC]
    return outf


def kernel(L0_attn_x, L0_attn_Wq, L0_attn_bq, L0_attn_Wk, L0_attn_bk,
           L0_attn_Wv, L0_attn_bv, L0_attn_Wo, L0_attn_bo,
           L0_ffn_Wr, L0_ffn_Wg, L0_ffn_Wu, L0_ffn_Wd):
    in_maps = make_in_maps(L0_attn_x, L0_attn_Wq, L0_attn_bq, L0_attn_Wk,
                           L0_attn_bk, L0_attn_Wv, L0_attn_bv, L0_attn_Wo,
                           L0_attn_bo, L0_ffn_Wr, L0_ffn_Wg, L0_ffn_Wu, L0_ffn_Wd)
    nc = _get_nc()
    res = run_bass_kernel_spmd(nc, in_maps, core_ids=list(range(NCORE)))
    return assemble_out(res.results)


# version 23
# speedup vs baseline: 1.0355x; 1.0355x over previous
import contextlib
import numpy as np
import ml_dtypes
import concourse.bass as bass
import concourse.mybir as mybir
from concourse.bass_utils import run_bass_kernel_spmd

F32 = mybir.dt.float32
BF16 = mybir.dt.bfloat16
AF = mybir.ActivationFunctionType
ALU = mybir.AluOpType
AX = mybir.AxisListType
BF = ml_dtypes.bfloat16

D = 2048
DK = 128
H = 16
S = 8192
T = 2048
NB = 2
NCORE = 8
NCHUNK = 4
QC = 512
EPS = 1e-5
NEXP = 8
CAP = 1024
SEGS = [(o, min(512, CAP - o)) for o in range(0, CAP, 512)]
MOE = "sparse"
SCALE = float(1.0 / np.sqrt(DK))

ENGINES = ("sync", "tensor", "vector", "scalar", "gpsimd")


class Sem:
    __slots__ = ("name", "h", "n")

    def __init__(self, name):
        self.name, self.h, self.n = name, None, 0


class Eng:
    def __init__(self, prog, name, self_sem):
        self.prog, self.name, self.self_sem = prog, name, self_sem
        self.q = []

    def wait(self, sem, val=None):
        v = sem.n if val is None else val
        if v > 0:
            self.q.append(("w", sem, v))

    def op(self, opname, *args, inc=None, chain=True, **kw):
        incs = []
        if chain and self.self_sem is not None:
            self.wait(self.self_sem)
            self.self_sem.n += 1
            incs.append((self.self_sem, 1))
        if inc is not None:
            sem, amt = inc
            sem.n += amt
            incs.append((sem, amt))
        self.q.append(("o", opname, args, kw, incs))

    def raw(self, fn):
        self.q.append(("c", fn))

    def dma(self, out, in_, sem, **kw):
        sem.n += 16
        self.q.append(("o", "dma_start", (), dict(out=out, in_=in_, **kw),
                       [(sem, 16)]))
        return sem.n

    def replay(self, eng):
        for it in self.q:
            if it[0] == "w":
                eng.wait_ge(it[1].h, it[2])
            elif it[0] == "c":
                it[1](eng)
            else:
                _, opname, args, kw, incs = it
                kw = {k: (v() if callable(v) and getattr(v, "_late", False) else v)
                      for k, v in kw.items()}
                ins = getattr(eng, opname)(*args, **kw)
                for sem, amt in incs:
                    ins.then_inc(sem.h, amt)


class Prog:
    def __init__(self):
        self.sems = {}
        self.engs = {}
        for e in ENGINES:
            ss = self._sem("chain_" + e) if e in ("vector", "scalar", "gpsimd") else None
            self.engs[e] = Eng(self, e, ss)

    def _sem(self, name):
        if name not in self.sems:
            self.sems[name] = Sem(name)
        return self.sems[name]


def build_nc(phases=("p1", "p2", "p3", "p4"), dbg=()):
    nc = bass.Bass("TRN2", target_bir_lowering=False, debug=False)
    ap = lambda t: t.ap()

    xq = ap(nc.dram_tensor("xq", [T, D], F32, kind="ExternalInput"))
    xkv = ap(nc.dram_tensor("xkv", [S, D], F32, kind="ExternalInput"))
    masks = ap(nc.dram_tensor("masks", [16, 128, QC], BF16, kind="ExternalInput"))
    WqT = ap(nc.dram_tensor("WqT", [D, D], F32, kind="ExternalInput"))
    WkT = ap(nc.dram_tensor("WkT", [D, D], F32, kind="ExternalInput"))
    WvT = ap(nc.dram_tensor("WvT", [D, D], F32, kind="ExternalInput"))
    WoT = ap(nc.dram_tensor("WoT", [D, D], F32, kind="ExternalInput"))
    bq = ap(nc.dram_tensor("bq", [D], F32, kind="ExternalInput")).rearrange("(a b) -> a b", b=1)
    bk = ap(nc.dram_tensor("bk", [D], F32, kind="ExternalInput")).rearrange("(a b) -> a b", b=1)
    bv = ap(nc.dram_tensor("bv", [D], F32, kind="ExternalInput")).rearrange("(a b) -> a b", a=1)
    bo = ap(nc.dram_tensor("bo", [D], F32, kind="ExternalInput")).rearrange("(a b) -> a b", a=1)
    Wr = ap(nc.dram_tensor("Wr", [D, NEXP], F32, kind="ExternalInput"))
    Wg = ap(nc.dram_tensor("Wg", [NEXP, D, D], F32, kind="ExternalInput"))
    Wu = ap(nc.dram_tensor("Wu", [NEXP, D, D], F32, kind="ExternalInput"))
    Wd = ap(nc.dram_tensor("Wd", [NEXP, D, D], F32, kind="ExternalInput"))
    out = ap(nc.dram_tensor("out", [T + 128, D], F32, kind="ExternalOutput"))

    WgB = ap(nc.dram_tensor("WgB", [NEXP, D, D], BF16))
    WuB = ap(nc.dram_tensor("WuB", [NEXP, D, D], BF16))
    WdB = ap(nc.dram_tensor("WdB", [NEXP, D, D], BF16))
    H2TM = ap(nc.dram_tensor("H2TM", [T + 128, D], BF16))
    GATES_D = ap(nc.dram_tensor("GATES_D", [T + 128, NEXP], F32))
    IDXTOK = ap(nc.dram_tensor("IDXTOK", [NEXP * CAP + 128, 1], mybir.dt.int32))
    Kscr = ap(nc.dram_tensor("Kscr", [H, DK, S], BF16))
    Vscr = ap(nc.dram_tensor("Vscr", [H, 128, S // 128, DK], BF16))
    Qscr = ap(nc.dram_tensor("Qscr", [H, DK, T], BF16))
    CTX = ap(nc.dram_tensor("CTXscr", [H, DK, T], BF16))
    LN2F = ap(nc.dram_tensor("LN2F", [D // 128, 128, T], BF16))

    dbg_outs = []
    for name, src, shape in (
        ("kscr", Kscr, [H, DK, S]), ("vscr", Vscr, [H, 128, S // 128, DK]),
        ("qscr", Qscr, [H, DK, T]), ("ctx", CTX, [H, DK, T]),
        ("ln2f", LN2F, [D // 128, 128, T]),
    ):
        if name in dbg:
            d = ap(nc.dram_tensor("d_" + name, shape, BF16, kind="ExternalOutput"))
            dbg_outs.append((d, src))

    identc = ap(nc.inline_tensor(np.eye(128, dtype=BF), name="identc"))
    ltc = ap(nc.inline_tensor(np.tril(np.ones((128, 128), dtype=np.float32), -1).astype(BF), name="ltc"))
    tokidc = ap(nc.inline_tensor(
        (np.arange(128)[:, None] + 128 * np.arange(16)[None, :]).astype(np.int32), name="tokidc"))
    offsc = ap(nc.inline_tensor((CAP * np.arange(NEXP, dtype=np.float32))[None, :].copy(), name="offsc"))
    idxinitc = ap(nc.inline_tensor(
        (T + (np.arange(NEXP * CAP + 128) % 128)).astype(np.int32)[:, None].copy(), name="idxinitc"))
    bigpc = ap(nc.inline_tensor(
        (NEXP * CAP + np.arange(128, dtype=np.float32))[:, None].copy(), name="bigpc"))
    capendc = ap(nc.inline_tensor(
        np.broadcast_to((CAP * (np.arange(NEXP, dtype=np.float32) + 1))[None, :], (128, NEXP)).copy(),
        name="capendc"))

    P = Prog()
    SY, TE, VE, SC, GP = (P.engs[e] for e in ENGINES)
    sm = P._sem
    s_w, s_kv, s_at, s_ct, s_o, s_ms = (
        sm("s_w"), sm("s_kv"), sm("s_at"), sm("s_ct"), sm("s_o"), sm("s_ms"))
    s_x01 = [sm("s_x0"), sm("s_x1")]
    s_ev = [sm("s_ev0"), sm("s_ev1"), sm("s_ev2"), sm("s_ev3")]
    s_e2 = [sm("s_e20"), sm("s_e21")]
    s_h = [sm("s_h0"), sm("s_h1")]
    s_mk = sm("s_mk")
    s_h2 = sm("s_h2")
    s_tr = sm("s_tr")
    s_ic = sm("s_ic")
    s_sc01 = [sm("s_sc0"), sm("s_sc1")]
    s_idx = sm("s_idx")
    s_g = [sm("s_g0"), sm("s_g1")]
    s_gc = [sm("s_gc0"), sm("s_gc1")]
    s_wd = [sm("s_wd0"), sm("s_wd1")]
    s_cl = [sm("s_cl0"), sm("s_cl1")]
    s_cs = [sm("s_cs0"), sm("s_cs1")]
    s_sa = [sm(f"s_sa{h_}{p_}{ep_}") for h_ in range(4) for p_ in range(2)
            for ep_ in range(2)]
    pe = sm("pe")
    ve = sm("chain_vector")
    ac = sm("chain_scalar")
    gp = sm("chain_gpsimd")

    es = contextlib.ExitStack()

    def sb(stack, name, shape, dtype):
        return stack.enter_context(nc.sbuf_tensor(name, shape, dtype))

    with es:
        ident = sb(es, "ident", [128, 128], BF16)
        ones_mat = sb(es, "ones_mat", [128, 128], BF16)
        ones_col = sb(es, "ones_col", [128, 1], BF16)
        ones_rowb = sb(es, "ones_rowb", [1, 128], BF16)
        ones_rowf = sb(es, "ones_rowf", [1, 128], F32)
        bq_sb = sb(es, "bq_sb", [128, 16], F32)
        bk_sb = sb(es, "bk_sb", [128, 16], F32)
        bqs_sb = sb(es, "bqs_sb", [128, 16], F32)
        bv_row = sb(es, "bv_row", [1, D], BF16)
        wr_sb = sb(es, "wr_sb", [128, 16 * NEXP], BF16)
        Wbuf = sb(es, "Wbuf", [128, 16 * D], BF16)
        evb = sb(es, "evb", [128, 4 * 512], BF16)
        gates = sb(es, "gates", [128, 16 * NEXP], F32)
        colA = sb(es, "colA", [128, 1], F32)
        colB = sb(es, "colB", [128, 1], F32)
        colC = sb(es, "colC", [128, 1], F32)
        colD = sb(es, "colD", [128, 1], F32)
        scol = sb(es, "scol", [128, 1], F32)

        ps_a = es.enter_context(nc.psum_tensor("ps_a", [128, 512], F32))
        ps_b = es.enter_context(nc.psum_tensor("ps_b", [128, 512], F32))
        ps_c = es.enter_context(nc.psum_tensor("ps_c", [128, 512], F32))
        ps_d = es.enter_context(nc.psum_tensor("ps_d", [128, 512], F32))
        ps_e = es.enter_context(nc.psum_tensor("ps_e", [128, 512], F32))
        ps_f = es.enter_context(nc.psum_tensor("ps_f", [128, 512], F32))
        ps_t1 = es.enter_context(nc.psum_tensor("ps_t1", [128, 1024], BF16))
        ps_t2 = es.enter_context(nc.psum_tensor("ps_t2", [128, 1024], BF16))

        SY.dma(ident[:], identc[:], s_ms)
        VE.op("memset", ones_mat[:], 1.0)
        VE.op("memset", ones_col[:], 1.0)
        VE.op("memset", ones_rowb[:], 1.0)
        VE.op("memset", ones_rowf[:], 1.0)
        for c in range(16):
            SY.dma(bq_sb[:, c:c + 1], bq[c * 128:(c + 1) * 128, :], s_ms)
            SY.dma(bk_sb[:, c:c + 1], bk[c * 128:(c + 1) * 128, :], s_ms)
            GP.dma(wr_sb[:, c * NEXP:(c + 1) * NEXP], Wr[c * 128:(c + 1) * 128, :], s_w)
        GP.dma(bv_row[:], bv[:], s_w)
        VE.wait(s_ms)
        VE.wait(s_w)
        VE.op("tensor_scalar_mul", bqs_sb[:], bq_sb[:], SCALE)
        VE.op("memset", scol[:], SCALE)

        if "p1" in phases:
            p1 = contextlib.ExitStack()
            with p1:
                Wbuf2 = sb(p1, "Wbuf2", [128, 16 * D], BF16)
                evb2 = sb(p1, "evb2", [128, 2 * 512], BF16)
                sqb = sb(p1, "sqb", [128, D // 2], F32)
                xg = sb(p1, "xg", [128, 2 * D], F32)
                lnt = sb(p1, "lnt", [128, 2 * D], BF16)
                lnf2 = [sb(p1, "lnf0", [128, 16 * 512], BF16),
                        sb(p1, "lnf1", [128, 16 * 512], BF16)]

                for c in range(16):
                    GP.dma(Wbuf[:, c * D:(c + 1) * D], WkT[c * 128:(c + 1) * 128, :], s_w)
                    GP.dma(Wbuf2[:, c * D:(c + 1) * D], WvT[c * 128:(c + 1) * 128, :], s_w)
                w_ready_kv = s_w.n

                ev_slot_dma = [0, 0, 0, 0]
                ev2_slot_dma = [0, 0]
                xg_slot_user = [0, 0]
                xg_slot_ac = [0, 0]

                def ln_tile_to_fm(src_dram, row0, tile_idx4, xslot):
                    SY.wait(ve, xg_slot_user[xslot])
                    SY.wait(ac, xg_slot_ac[xslot])
                    sx = s_x01[xslot]
                    SY.dma(xg[:, xslot * D:(xslot + 1) * D], src_dram[row0:row0 + 128, :], sx)
                    xv = sx.n
                    xt_ = xg[:, xslot * D:(xslot + 1) * D]
                    VE.wait(sx, xv)
                    VE.op("reduce_sum", colA[:], xt_, axis=AX.X)
                    SC.wait(sx, xv)
                    SC.wait(ve)
                    SC.op("activation", sqb[:], xt_[:, 0:D // 2], AF.Square)
                    VE.wait(ac)
                    VE.op("reduce_sum", colB[:], sqb[:], axis=AX.X)
                    SC.wait(ve)
                    SC.op("activation", sqb[:], xt_[:, D // 2:D], AF.Square)
                    VE.wait(ac)
                    VE.op("reduce_sum", colC[:], sqb[:], axis=AX.X)
                    VE.op("tensor_add", colB[:], colB[:], colC[:])
                    VE.op("tensor_scalar_mul", colA[:], colA[:], 1.0 / D)
                    VE.op("tensor_scalar_mul", colB[:], colB[:], 1.0 / D)
                    VE.op("tensor_mul", colC[:], colA[:], colA[:])
                    VE.op("tensor_scalar", colB[:], colB[:], colC[:, 0:1], EPS,
                          ALU.subtract, ALU.add)
                    SC.wait(ve)
                    SC.op("activation", colC[:], colB[:], AF.Ln)
                    SC.op("activation", colD[:], colC[:], AF.Exp, scale=-0.5)
                    VE.wait(ac)
                    lslot = tile_idx4 % 2
                    VE.wait(pe, lnt_tp[lslot])
                    VE.op("tensor_scalar", lnt[:, lslot * D:(lslot + 1) * D], xt_,
                          colA[:, 0:1], colD[:, 0:1], ALU.subtract, ALU.mult)
                    xg_slot_user[xslot] = ve.n
                    xg_slot_ac[xslot] = ac.n
                    lnf = lnf2[gpar[0]]
                    ts_done = ve.n
                    TE.wait(ve, ts_done)
                    TE.wait(pe, lnf_used[gpar[0]])
                    for d in range(16):
                        tb = [ps_t1, ps_t2][d % 2]
                        TE.wait(ac, tp_ac[d % 2])
                        TE.op("transpose", tb[:, 0:128],
                              lnt[:, lslot * D + d * 128: lslot * D + (d + 1) * 128],
                              ident[:], inc=(pe, 1))
                        tp_pe[d % 2] = pe.n
                        SC.wait(pe, tp_pe[d % 2])
                        SC.op("activation",
                              lnf[:, d * 512 + tile_idx4 * 128: d * 512 + (tile_idx4 + 1) * 128],
                              tb[:, 0:128], AF.Identity)
                        tp_ac[d % 2] = ac.n
                    lnt_tp[lslot] = pe.n

                lnt_tp = [0, 0]
                tp_ac = [0, 0]
                tp_pe = [0, 0]
                gpar = [0]
                lnf_used = [0, 0]
                grp_tp = [0, 0]

                def group_fm(src_dram, g0):
                    for i in range(4):
                        ln_tile_to_fm(src_dram, g0 + i * 128, i, i % 2)
                    grp_tp[gpar[0]] = ac.n

                def proj_fm(Wb, bias_col, scale, dst, g, bank_sel, nchunks=16):
                    lnf = lnf2[gpar[0]]
                    for e in range(nchunks):
                        bank = bank_sel[e % 2]
                        TE.wait(ac, grp_tp[gpar[0]])
                        TE.wait(ac, bank["ac"])
                        for d in range(16):
                            TE.op("matmul", bank["ps"][:],
                                  Wb[:, d * D + e * 128: d * D + (e + 1) * 128],
                                  lnf[:, d * 512:(d + 1) * 512],
                                  start=(d == 0), stop=(d == 15),
                                  inc=(pe, 1) if d == 15 else None)
                        bank["pe"] = pe.n
                        slot = e % 4
                        SC.wait(pe, bank["pe"])
                        SC.wait(s_ev[slot], ev_slot_dma[slot])
                        SC.op("activation", evb[:, slot * 512:(slot + 1) * 512],
                              bank["ps"][:], AF.Identity,
                              bias=bias_col(e), scale=scale)
                        bank["ac"] = ac.n
                        SY.wait(ac, bank["ac"])
                        ev_slot_dma[slot] = SY.dma(dst(e, g), evb[:, slot * 512:(slot + 1) * 512], s_ev[slot])

                bankA = {"ps": ps_a, "pe": 0, "ac": 0}
                bankB = {"ps": ps_b, "pe": 0, "ac": 0}
                bankC = {"ps": ps_c, "pe": 0, "ve": 0}
                bankD = {"ps": ps_d, "pe": 0, "ve": 0}

                for g in range(16):
                    group_fm(xkv, g * 512)
                    if g == 0:
                        TE.wait(s_w, w_ready_kv)
                    lnf = lnf2[gpar[0]]
                    proj_fm(Wbuf, lambda e: bk_sb[:, e:e + 1], 1.0,
                            lambda e, gg=g: Kscr[e, :, gg * 512:(gg + 1) * 512],
                            g, [bankA, bankB])
                    vsel = [bankC, bankD]
                    vidx = 0
                    for eg in range(4):
                        for i in range(4):
                            bank = vsel[vidx % 2]
                            slot = vidx % 2
                            vidx += 1
                            TE.wait(ve, bank["ve"])
                            for d in range(16):
                                TE.op("matmul", bank["ps"][:],
                                      lnf[:, d * 512 + i * 128: d * 512 + (i + 1) * 128],
                                      Wbuf2[:, d * D + eg * 512: d * D + (eg + 1) * 512],
                                      start=(d == 0), stop=False)
                            TE.op("matmul", bank["ps"][:], ones_rowb[:],
                                  bv_row[:, eg * 512:(eg + 1) * 512],
                                  start=False, stop=True, inc=(pe, 1))
                            bank["pe"] = pe.n
                            VE.wait(pe, bank["pe"])
                            VE.wait(s_e2[slot], ev2_slot_dma[slot])
                            VE.op("tensor_copy", evb2[:, slot * 512:(slot + 1) * 512], bank["ps"][:])
                            bank["ve"] = ve.n
                            SY.wait(ve, bank["ve"])
                            for hh in range(4):
                                ev2_slot_dma[slot] = SY.dma(
                                    Vscr[eg * 4 + hh, :, g * 4 + i, :],
                                    evb2[:, slot * 512 + hh * 128: slot * 512 + (hh + 1) * 128],
                                    s_e2[slot])
                    lnf_used[gpar[0]] = pe.n
                    gpar[0] ^= 1

                pe_b4 = pe.n
                for c in range(16):
                    GP.wait(pe, pe_b4)
                    GP.dma(Wbuf[:, c * D:(c + 1) * D], WqT[c * 128:(c + 1) * 128, :], s_w)
                w_ready_q = s_w.n
                for g in range(4):
                    group_fm(xq, g * 512)
                    if g == 0:
                        TE.wait(s_w, w_ready_q)
                    proj_fm(Wbuf, lambda e: bqs_sb[:, e:e + 1], scol[:, 0:1],
                            lambda e, gg=g: Qscr[e, :, gg * 512:(gg + 1) * 512],
                            g, [bankA, bankB])
                    lnf_used[gpar[0]] = pe.n
                    gpar[0] ^= 1

        if "p2" in phases:
            p2 = contextlib.ExitStack()
            with p2:
                Kh = [sb(p2, "Kh0", [128, S], BF16), sb(p2, "Kh1", [128, S], BF16)]
                Vh = [sb(p2, "Vh0", [128, S], BF16), sb(p2, "Vh1", [128, S], BF16)]
                Qh = [sb(p2, "Qh0", [128, T], BF16), sb(p2, "Qh1", [128, T], BF16)]
                msk = sb(p2, "msk", [128, 16 * QC], BF16)
                sqs = sb(p2, "sqs", [128, 2 * 2048], BF16)
                cstg = sb(p2, "cstg", [128, 2 * 2048], BF16)
                pt = sb(p2, "pt", [128, 4 * QC], BF16)
                ctxh = sb(p2, "ctxh", [128, T], BF16)
                den_row = sb(p2, "den_row", [1, 2 * QC], F32)
                rec_row = sb(p2, "rec_row", [1, 2 * QC], F32)
                recb = sb(p2, "recb", [128, QC], F32)
                cxe = sb(p2, "cxe", [128, 2 * QC], F32)
                negc = sb(p2, "negc", [128, 1], F32)
                kq2 = sb(p2, "kq2", [128, 1], F32)

                SY.wait(s_ev[0])
                SY.wait(s_ev[1])
                SY.wait(s_ev[2])
                SY.wait(s_ev[3])
                SY.wait(s_e2[0])
                SY.wait(s_e2[1])
                head_mm_done = [0, 0]
                ld_done = [0, 0]
                SY.dma(msk[:].rearrange("p (j q) -> p j q", q=QC),
                       masks.rearrange("j p q -> p j q"), s_mk)
                mv = s_mk.n
                SY.dma(Kh[0][:], Kscr[0], s_h[0])
                SY.dma(Vh[0][:].rearrange("p (t j) -> p t j", j=DK), Vscr[0], s_h[0])
                SY.dma(Qh[0][:], Qscr[0], s_h[0])
                ld_done[0] = s_h[0].n
                ctx_dma = [0]
                st2 = {"ctx_free": 0, "den_free": 0,
                       "recb_user": 0, "rec_user_pe": 0, "cxe_user": [0, 0]}
                pend = []

                cchunks = []
                for e_ in range(NEXP):
                    for src_, dst_ in ((Wg, WgB), (Wu, WuB), (Wd, WdB)):
                        for c_ in range(16):
                            cchunks.append((src_, dst_, e_, c_))
                cv = {"load_i": 0, "store_i": 0, "cl_v": [0, 0], "cs_g": [0, 0]}
                p1_quiesce = (pe.n, ac.n, ve.n)

                def emit_conv(n):
                    for _ in range(n):
                        j = cv["store_i"]
                        if j >= len(cchunks):
                            return
                        while cv["load_i"] <= min(j + 1, len(cchunks) - 1):
                            li = cv["load_i"]
                            slot = li % 2
                            src_, dst_, e_, c_ = cchunks[li]
                            if cv["load_i"] == 0:
                                GP.wait(pe, p1_quiesce[0])
                                GP.wait(ac, p1_quiesce[1])
                                GP.wait(ve, p1_quiesce[2])
                                for s__ in s_ev:
                                    GP.wait(s__)
                                GP.wait(s_e2[0])
                                GP.wait(s_e2[1])
                                GP.wait(s_x01[0])
                                GP.wait(s_x01[1])
                            GP.wait(s_cs[slot], cv["cs_g"][slot])
                            GP.dma(cstg[:, slot * 2048:(slot + 1) * 2048],
                                   src_[e_, c_ * 128:(c_ + 1) * 128, :], s_cl[slot])
                            cv["cl_v"][slot] = s_cl[slot].n
                            cv["load_i"] += 1
                        slot = j % 2
                        src_, dst_, e_, c_ = cchunks[j]
                        GP.wait(s_cl[slot], cv["cl_v"][slot])
                        GP.dma(dst_[e_, c_ * 128:(c_ + 1) * 128, :],
                               cstg[:, slot * 2048:(slot + 1) * 2048], s_cs[slot])
                        cv["cs_g"][slot] = s_cs[slot].n
                        cv["store_i"] += 1

                for h in range(H):
                    sl = h % 2
                    Khc, Vhc, Qhc = Kh[sl], Vh[sl], Qh[sl]
                    emit_conv(28)
                    at_ready = ld_done[sl]
                    if h + 1 < H:
                        nsl = (h + 1) % 2
                        SY.wait(pe, head_mm_done[nsl])
                        SY.dma(Kh[nsl][:], Kscr[h + 1], s_h[nsl])
                        SY.dma(Vh[nsl][:].rearrange("p (t j) -> p t j", j=DK), Vscr[h + 1], s_h[nsl])
                        SY.dma(Qh[nsl][:], Qscr[h + 1], s_h[nsl])
                        ld_done[nsl] = s_h[nsl].n

                    SC.wait(s_h[sl], at_ready)
                    TE.wait(ac)
                    TE.wait(ve)
                    nrm_ve = [0, 0]
                    sq_pe = [0, 0]
                    sqi = 0
                    for which, buf, width in (("q", Qhc, T), ("k", Khc, S)):
                        first = True
                        for idx in range(width // 2048):
                            sslot = sqi % 2
                            sqi += 1
                            SC.wait(pe, sq_pe[sslot])
                            SC.op("activation", sqs[:, sslot * 2048:(sslot + 1) * 2048],
                                  buf[:, idx * 2048:(idx + 1) * 2048], AF.Square)
                            sq_ac = ac.n
                            for q4 in range(4):
                                bsel = q4 % 2
                                bank = [ps_a, ps_b][bsel]
                                TE.wait(ac, sq_ac)
                                TE.wait(ve, nrm_ve[bsel])
                                TE.op("matmul", bank[:], ones_mat[:],
                                      sqs[:, sslot * 2048 + q4 * 512: sslot * 2048 + (q4 + 1) * 512],
                                      start=True, stop=True, inc=(pe, 1))
                                mmv = pe.n
                                VE.wait(pe, mmv)
                                VE.op("reduce_max", colA[:], bank[:], axis=AX.X)
                                if first:
                                    VE.op("tensor_copy", colB[:], colA[:])
                                    first = False
                                else:
                                    VE.op("tensor_max", colB[:], colB[:], colA[:])
                                nrm_ve[bsel] = ve.n
                            sq_pe[sslot] = pe.n
                        if which == "q":
                            VE.op("tensor_copy", colC[:], colB[:])
                    VE.op("tensor_mul", kq2[:], colC[:], colB[:])
                    SC.wait(ve)
                    SC.op("activation", colD[:], kq2[:], AF.Ln)
                    SC.op("activation", negc[:], colD[:], AF.Exp, scale=0.5)
                    VE.wait(ac)
                    VE.op("tensor_scalar_mul", negc[:], negc[:], float(-1.02 * SCALE))
                    negc_ve = ve.n
                    TE.wait(ve, negc_ve)

                    def finish_norm(pd):
                        k0, slot, rec_ready = pd
                        TE.wait(ve, rec_ready)
                        TE.wait(ve, st2["recb_user"])
                        TE.op("matmul", ps_e[:], ones_rowf[:], rec_row[:, slot * QC:(slot + 1) * QC],
                              start=True, stop=True, inc=(pe, 1))
                        st2["rec_user_pe"] = pe.n
                        VE.wait(pe, st2["rec_user_pe"])
                        VE.op("tensor_copy", recb[:], ps_e[:])
                        VE.wait(s_ct, ctx_dma[0])
                        VE.op("tensor_mul", ctxh[:, k0 * QC:(k0 + 1) * QC],
                              cxe[:, slot * QC:(slot + 1) * QC], recb[:])
                        st2["recb_user"] = ve.n
                        st2["cxe_user"][slot] = ve.n

                    for k in range(NCHUNK):
                        ntile = 16 * (k + 1)
                        sbank = [{"ps": ps_a, "ac": 0}, {"ps": ps_b, "ac": 0}, {"ps": ps_f, "ac": 0}]
                        pt_free = [0, 0, 0, 0]
                        exp_ac = [0] * ntile
                        mask_ve = [0] * ntile
                        sj_pe = [0] * ntile
                        for step in range(ntile + 1):
                            if step < ntile:
                                j = step
                                bk_ = sbank[j % 3]
                                TE.wait(ac, bk_["ac"])
                                TE.op("matmul", bk_["ps"][:],
                                      Khc[:, j * 128:(j + 1) * 128],
                                      Qhc[:, k * QC:(k + 1) * QC],
                                      start=True, stop=True, inc=(pe, 1))
                                sj_pe[j] = pe.n
                                ptt = pt[:, (j % 4) * QC:(j % 4 + 1) * QC]
                                SC.wait(pe, max(sj_pe[j], pt_free[j % 4]))
                                SC.wait(ve, negc_ve)
                                SC.op("activation", ptt, bk_["ps"][:], AF.Exp, bias=negc[:, 0:1])
                                exp_ac[j] = ac.n
                                bk_["ac"] = ac.n
                                if j >= ntile - 16:
                                    m = j - (ntile - 16)
                                    VE.wait(s_mk, mv)
                                    VE.wait(ac, exp_ac[j])
                                    VE.op("tensor_mul", ptt, ptt, msk[:, m * QC:(m + 1) * QC])
                                    mask_ve[j] = ve.n
                            if step == 4 and len(pend) == 2:
                                finish_norm(pend.pop(0))
                            if step >= 1:
                                i = step - 1
                                ptt = pt[:, (i % 4) * QC:(i % 4 + 1) * QC]
                                TE.wait(ac, exp_ac[i])
                                if mask_ve[i]:
                                    TE.wait(ve, mask_ve[i])
                                if i == 0:
                                    TE.wait(ve, st2["ctx_free"])
                                    TE.wait(ve, st2["den_free"])
                                TE.op("matmul", ps_d[0:1, :], ones_col[:], ptt,
                                      start=(i == 0), stop=(i == ntile - 1))
                                TE.op("matmul", ps_c[:], Vhc[:, i * 128:(i + 1) * 128], ptt,
                                      start=(i == 0), stop=(i == ntile - 1), inc=(pe, 1))
                                pt_free[i % 4] = pe.n
                        head_mm_done[sl] = pe.n
                        slot = k % 2
                        VE.wait(pe)
                        VE.op("tensor_copy", cxe[:, slot * QC:(slot + 1) * QC], ps_c[:])
                        st2["ctx_free"] = ve.n
                        VE.op("tensor_copy", den_row[:, slot * QC:(slot + 1) * QC], ps_d[0:1, :])
                        st2["den_free"] = ve.n
                        VE.wait(pe, st2["rec_user_pe"])
                        VE.op("reciprocal", rec_row[:, slot * QC:(slot + 1) * QC],
                              den_row[:, slot * QC:(slot + 1) * QC])
                        pend.append((k, slot, ve.n))
                    while pend:
                        finish_norm(pend.pop(0))
                    SY.wait(ve)
                    ctx_dma[0] = SY.dma(CTX[h], ctxh[:], s_ct)

        if "p3" in phases:
            p3 = contextlib.ExitStack()
            with p3:
                ctxfull = sb(p3, "ctxfull", [128, H * T], BF16)
                bo_row = sb(p3, "bo_row", [1, D], BF16)
                sqb = sb(p3, "sqb3", [128, D // 2], F32)
                xqt = sb(p3, "xqt", [128, 2 * D], F32)
                x2t = sb(p3, "x2t", [128, D], F32)
                h2t = sb(p3, "h2t", [128, D], BF16)
                trb = sb(p3, "trb", [128, D], BF16)
                logit = sb(p3, "logit", [128, NEXP], F32)
                tcC = sb(p3, "tcC", [128, NEXP], F32)
                tcD = sb(p3, "tcD", [128, NEXP], F32)
                tcE = sb(p3, "tcE", [128, NEXP], F32)
                if MOE == "sparse":
                    lt_sb = sb(p3, "lt_sb", [128, 128], BF16)
                    tokid_sb = sb(p3, "tokid_sb", [128, 16], mybir.dt.int32)
                    offs_sb = sb(p3, "offs_sb", [1, NEXP], F32)
                    capend_sb = sb(p3, "capend_sb", [128, NEXP], F32)
                    selx = sb(p3, "selx", [128, NEXP], F32)
                    selb = sb(p3, "selb", [128, NEXP], BF16)
                    pfx = sb(p3, "pfx", [128, NEXP], F32)
                    slotf = sb(p3, "slotf", [128, NEXP], F32)
                    slot_i = sb(p3, "slot_i", [128, 2 * NEXP], mybir.dt.int32)
                    bigp_sb = sb(p3, "bigp_sb", [128, 1], F32)
                    totrow = sb(p3, "totrow", [1, NEXP], F32)
                    baserow = sb(p3, "baserow", [1, NEXP], F32)
                    boff = sb(p3, "boff", [1, NEXP], F32)

                pe_b4 = pe.n
                GP.wait(pe, pe_b4)
                GP.dma(bo_row[:], bo[:], s_w)
                for c in range(16):
                    GP.wait(pe, pe_b4)
                    GP.dma(Wbuf[:, c * D:(c + 1) * D], WoT[c * 128:(c + 1) * 128, :], s_w)
                w_ready_o = s_w.n
                trash_kv0 = [0]
                if MOE == "sparse":
                    SY.dma(lt_sb[:], ltc[:], s_ms)
                    SY.dma(tokid_sb[:], tokidc[:], s_ms)
                    SY.dma(offs_sb[:], offsc[:], s_ms)
                    SY.dma(capend_sb[:], capendc[:], s_ms)
                    SY.dma(bigp_sb[:], bigpc[:], s_ms)
                    VE.op("memset", baserow[:], 0.0)
                    VE.wait(s_ms)
                    gp_init = GP.dma(IDXTOK[:], idxinitc[:], s_ic)
                    VE.op("memset", tcC[:], 0.0)
                    VE.op("memset", h2t[:], 0.0)
                    trash_ve = ve.n
                    GP.wait(ve, trash_ve)
                    GP.dma(H2TM[T:T + 128, :], h2t[:], s_tr)
                    GP.dma(GATES_D[T:T + 128, :], tcC[:], s_tr)
                    trash_kv0[0] = s_tr.n
                    GP.wait(s_ic, gp_init)
                SY.wait(s_ct)
                for hh in range(H):
                    SY.dma(ctxfull[:, hh * T:(hh + 1) * T], CTX[hh], s_at)
                ctx_ready = s_at.n
                xq_dma = [0, 0]
                xq_user = [0, 0]
                SY.dma(xqt[:, 0:D], xq[0:128, :], s_x01[0])
                xq_dma[0] = s_x01[0].n
                ob = [{"ps": ps_a, "ve": 0}, {"ps": ps_b, "ve": 0}]
                oidx = 0
                x2_dma = [0]
                h2_tp = [0]
                tr_dma = [0]
                x2_sc = [0]
                rtr_pe = [0]
                tp_ac3 = [0, 0]
                tp_pe3 = [0, 0]
                h2tm_dma = [0]
                idx_ve = [0]
                slot_sc = [0, 0]

                for i in range(16):
                    if i + 1 < 16:
                        SY.wait(ve, xq_user[(i + 1) % 2])
                        SY.dma(xqt[:, ((i + 1) % 2) * D:((i + 1) % 2 + 1) * D],
                               xq[(i + 1) * 128:(i + 2) * 128, :], s_x01[(i + 1) % 2])
                        xq_dma[(i + 1) % 2] = s_x01[(i + 1) % 2].n
                    if i == 0:
                        TE.wait(s_w, w_ready_o)
                        TE.wait(s_at, ctx_ready)
                    for eg in range(4):
                        bk_ = ob[oidx % 2]
                        oidx += 1
                        TE.wait(ve, bk_["ve"])
                        for cd in range(16):
                            TE.op("matmul", bk_["ps"][:],
                                  ctxfull[:, cd * T + i * 128: cd * T + (i + 1) * 128],
                                  Wbuf[:, cd * D + eg * 512: cd * D + (eg + 1) * 512],
                                  start=(cd == 0), stop=False)
                        TE.op("matmul", bk_["ps"][:], ones_rowb[:],
                              bo_row[:, eg * 512:(eg + 1) * 512],
                              start=False, stop=True, inc=(pe, 1))
                        VE.wait(pe)
                        VE.wait(s_x01[i % 2], xq_dma[i % 2])
                        VE.wait(s_o, x2_dma[0])
                        VE.wait(ac, x2_sc[0])
                        VE.op("tensor_add", x2t[:, eg * 512:(eg + 1) * 512], bk_["ps"][:],
                              xqt[:, (i % 2) * D + eg * 512:(i % 2) * D + (eg + 1) * 512])
                        bk_["ve"] = ve.n
                    xq_user[i % 2] = ve.n
                    SY.wait(ve)
                    x2_dma[0] = SY.dma(out[i * 128:(i + 1) * 128, :], x2t[:], s_o)
                    VE.op("reduce_sum", colA[:], x2t[:], axis=AX.X)
                    SC.wait(ve)
                    SC.op("activation", sqb[:], x2t[:, 0:D // 2], AF.Square)
                    VE.wait(ac)
                    VE.op("reduce_sum", colB[:], sqb[:], axis=AX.X)
                    SC.wait(ve)
                    SC.op("activation", sqb[:], x2t[:, D // 2:D], AF.Square)
                    VE.wait(ac)
                    VE.op("reduce_sum", colC[:], sqb[:], axis=AX.X)
                    VE.op("tensor_add", colB[:], colB[:], colC[:])
                    VE.op("tensor_scalar_mul", colA[:], colA[:], 1.0 / D)
                    VE.op("tensor_scalar_mul", colB[:], colB[:], 1.0 / D)
                    VE.op("tensor_mul", colC[:], colA[:], colA[:])
                    VE.op("tensor_scalar", colB[:], colB[:], colC[:, 0:1], EPS,
                          ALU.subtract, ALU.add)
                    SC.wait(ve)
                    SC.op("activation", colC[:], colB[:], AF.Ln)
                    SC.op("activation", colD[:], colC[:], AF.Exp, scale=-0.5)
                    x2_sc[0] = ac.n
                    VE.wait(ac)
                    VE.wait(pe, h2_tp[0])
                    VE.wait(s_h2, h2tm_dma[0])
                    VE.wait(s_tr, trash_kv0[0])
                    VE.op("tensor_scalar", h2t[:], x2t[:], colA[:, 0:1], colD[:, 0:1],
                          ALU.subtract, ALU.mult)
                    h2_ve = ve.n
                    TE.wait(ve, h2_ve)
                    TE.wait(pe, rtr_pe[0])
                    for d in range(16):
                        tb = [ps_t1, ps_t2][d % 2]
                        TE.wait(ac, tp_ac3[d % 2])
                        TE.op("transpose", tb[:, 0:128], h2t[:, d * 128:(d + 1) * 128],
                              ident[:], inc=(pe, 1))
                        tp_pe3[d % 2] = pe.n
                        SC.wait(pe, tp_pe3[d % 2])
                        SC.op("activation", trb[:, d * 128:(d + 1) * 128],
                              tb[:, 0:128], AF.Identity)
                        tp_ac3[d % 2] = ac.n
                    h2_tp[0] = pe.n
                    trb_ac = ac.n
                    TE.wait(ac, trb_ac)
                    TE.wait(ve)
                    for d in range(16):
                        TE.op("matmul", ps_f[:, 0:NEXP],
                              trb[:, d * 128:(d + 1) * 128],
                              wr_sb[:, d * NEXP:(d + 1) * NEXP],
                              start=(d == 0), stop=(d == 15),
                              inc=(pe, 1) if d == 15 else None)
                    rtr_pe[0] = pe.n
                    VE.wait(pe)
                    VE.op("tensor_copy", logit[:], ps_f[:, 0:NEXP])
                    VE.op("reduce_max", colA[:], logit[:], axis=AX.X)
                    VE.wait(s_tr, trash_kv0[0])
                    VE.op("tensor_single_scalar", tcC[:], logit[:], colA[:, 0:1], ALU.is_equal)
                    VE.op("scalar_tensor_tensor", tcD[:], tcC[:], -1e30, logit[:], ALU.mult, ALU.add)
                    VE.op("reduce_max", colB[:], tcD[:], axis=AX.X)
                    VE.op("tensor_sub", colC[:], colA[:], colB[:])
                    SC.wait(ve)
                    SC.op("activation", colD[:], colC[:], AF.Sigmoid)
                    VE.wait(ac)
                    VE.op("tensor_scalar", colC[:], colD[:], -1.0, 1.0, ALU.mult, ALU.add)
                    VE.op("tensor_single_scalar", tcD[:], logit[:], colB[:, 0:1], ALU.is_equal)
                    VE.op("tensor_scalar", tcE[:], tcC[:], -1.0, 1.0, ALU.mult, ALU.add)
                    VE.op("tensor_mul", tcD[:], tcD[:], tcE[:])
                    VE.op("tensor_scalar_mul", tcC[:], tcC[:], colD[:, 0:1])
                    VE.op("tensor_scalar_mul", tcD[:], tcD[:], colC[:, 0:1])
                    VE.op("tensor_add", gates[:, i * NEXP:(i + 1) * NEXP], tcC[:], tcD[:])
                    if MOE == "sparse":
                        gts = gates[:, i * NEXP:(i + 1) * NEXP]
                        gates_ve = ve.n
                        SY.wait(ve, gates_ve)
                        SY.dma(GATES_D[i * 128:(i + 1) * 128, :], gts, s_kv)
                        SY.wait(ve, h2_ve)
                        h2tm_dma[0] = SY.dma(H2TM[i * 128:(i + 1) * 128, :], h2t[:], s_h2)
                        VE.op("tensor_single_scalar", selx[:], gts, 0.0, ALU.is_gt)
                        VE.op("tensor_copy", selb[:], selx[:])
                        VE.op("tensor_add", boff[:], baserow[:], offs_sb[:])
                        sel_ve = ve.n
                        TE.wait(ve, sel_ve)
                        TE.wait(ve, idx_ve[0])
                        TE.op("matmul", ps_d[:, 0:NEXP], lt_sb[:], selb[:],
                              start=True, stop=True)
                        TE.op("matmul", ps_c[0:1, 0:NEXP], ones_col[:], selb[:],
                              start=True, stop=True)
                        TE.op("matmul", ps_e[:, 0:NEXP], ones_rowf[:], boff[:],
                              start=True, stop=True, inc=(pe, 1))
                        idx_pe = pe.n
                        spar = i % 2
                        VE.wait(pe, idx_pe)
                        VE.wait(s_sc01[spar], slot_sc[spar])
                        VE.op("tensor_copy", pfx[:], ps_d[:, 0:NEXP])
                        VE.op("tensor_add", pfx[:], pfx[:], ps_e[:, 0:NEXP])
                        VE.op("tensor_tensor", slotf[:], pfx[:], capend_sb[:], ALU.is_lt)
                        VE.op("tensor_mul", slotf[:], slotf[:], selx[:])
                        VE.op("tensor_scalar", pfx[:], pfx[:], bigp_sb[:, 0:1], None, ALU.subtract)
                        VE.op("tensor_mul", pfx[:], pfx[:], slotf[:])
                        VE.op("tensor_scalar", pfx[:], pfx[:], bigp_sb[:, 0:1], None, ALU.add)
                        VE.op("tensor_copy", slot_i[:, spar * NEXP:(spar + 1) * NEXP], pfx[:])
                        VE.op("tensor_copy", totrow[:], ps_c[0:1, 0:NEXP])
                        VE.op("tensor_add", baserow[:], baserow[:], totrow[:])
                        idx_ve[0] = ve.n
                        for e in range(NEXP):
                            GP.wait(ve, idx_ve[0])
                            GP.op("indirect_dma_start",
                                  out=IDXTOK[:], 
                                  out_offset=bass.IndirectOffsetOnAxis(
                                      ap=slot_i[:, spar * NEXP + e: spar * NEXP + e + 1], axis=0),
                                  in_=tokid_sb[:, i:i + 1],
                                  in_offset=None,
                                  inc=(s_sc01[spar], 16), chain=False)
                        slot_sc[spar] = s_sc01[spar].n

        if "p4" in phases and MOE == "sparse":
            NSG = CAP // 512
            NCT = CAP // 128
            p4 = contextlib.ExitStack()
            with p4:
                gfm = sb(p4, "gfm", [128, 16 * CAP], BF16)
                gusp = sb(p4, "gusp", [128, 16 * CAP], BF16)
                wdq = [sb(p4, "wdq0", [128, 16 * 512], BF16),
                       sb(p4, "wdq1", [128, 16 * 512], BF16)]
                gbuf = sb(p4, "gbuf", [128, 2 * D], BF16)
                hsc = sb(p4, "hsc", [128, 2 * 512], F32)
                idxg2 = [sb(p4, "idxg0", [128, NCT], mybir.dt.int32),
                         sb(p4, "idxg1", [128, NCT], mybir.dt.int32)]
                gcol2 = [sb(p4, "gcol0", [128, NCT], F32),
                         sb(p4, "gcol1", [128, NCT], F32)]

                gsel = [{"ps": ps_a, "ev": (ve, 0)}, {"ps": ps_b, "ev": (ve, 0)}]
                hsc_guard = [None, None]
                wdq_user = [0, 0]
                wq_rdy = [0, 0]
                first_accum = [True]
                gbuf_tp = [0, 0]
                tp_ac4 = [0, 0]
                tp_pe4 = [0, 0]
                gcol_user2 = [0, 0]

                SY.wait(s_sc01[0])
                SY.wait(s_sc01[1])
                SY.wait(s_kv)
                SY.wait(s_h2)
                SY.wait(s_tr)
                pe_b4 = pe.n
                GP.wait(s_cs[0])
                GP.wait(s_cs[1])
                for c in range(16):
                    GP.wait(pe, pe_b4)
                    GP.dma(Wbuf[:, c * D:(c + 1) * D], WgB[0, c * 128:(c + 1) * 128, :], s_w)
                wg_rdy = s_w.n
                TE.wait(ac)
                for e in range(NEXP):
                    es = e % 2
                    idxg = idxg2[es]
                    gcol = gcol2[es]
                    SY.wait(s_g[0])
                    SY.wait(s_g[1])
                    SY.wait(s_gc[0])
                    SY.wait(s_gc[1])
                    for h_ in range(4):
                        for p_ in range(2):
                            SY.wait(s_sa[h_ * 4 + p_ * 2 + es])
                    SY.wait(ve, gcol_user2[es])
                    for ct in range(NCT):
                        SY.dma(idxg[:, ct:ct + 1], IDXTOK[e * CAP + ct * 128: e * CAP + (ct + 1) * 128, :], s_idx)
                    idx_rdy = s_idx.n
                    for ct in range(NCT):
                        gsl = ct % 2
                        GP.wait(s_idx, idx_rdy)
                        GP.wait(pe, gbuf_tp[gsl])
                        GP.op("indirect_dma_start",
                              out=gbuf[:, gsl * D:(gsl + 1) * D],
                              out_offset=None,
                              in_=H2TM[:],
                              in_offset=bass.IndirectOffsetOnAxis(ap=idxg[:, ct:ct + 1], axis=0),
                              inc=(s_g[gsl], 16), chain=False)
                        GP.op("indirect_dma_start",
                              out=gcol[:, ct:ct + 1],
                              out_offset=None,
                              in_=GATES_D[:],
                              in_offset=bass.IndirectOffsetOnAxis(ap=idxg[:, ct:ct + 1], axis=0),
                              element_offset=e,
                              inc=(s_gc[ct % 2], 16), chain=False)
                        TE.wait(s_g[gsl], s_g[gsl].n)
                        TE.wait(s_gc[ct % 2], s_gc[ct % 2].n)
                        for d in range(16):
                            tb = [ps_t1, ps_t2][d % 2]
                            TE.wait(ac, tp_ac4[d % 2])
                            TE.op("transpose", tb[:, 0:128],
                                  gbuf[:, gsl * D + d * 128: gsl * D + (d + 1) * 128],
                                  ident[:], inc=(pe, 1))
                            tp_pe4[d % 2] = pe.n
                            SC.wait(pe, tp_pe4[d % 2])
                            SC.op("activation",
                                  gfm[:, d * CAP + ct * 128: d * CAP + (ct + 1) * 128],
                                  tb[:, 0:128], AF.Identity)
                            tp_ac4[d % 2] = ac.n
                        gbuf_tp[gsl] = pe.n
                    gfm_ac = ac.n
                    gidx = 0
                    for so, sw in SEGS:
                        for ic in range(16):
                            bk_ = gsel[gidx % 2]
                            gidx += 1
                            TE.wait(ac, gfm_ac)
                            if gidx == 1:
                                TE.wait(s_w, wg_rdy)
                            TE.wait(bk_["ev"][0], bk_["ev"][1])
                            for d in range(16):
                                TE.op("matmul", bk_["ps"][:, 0:sw],
                                      Wbuf[:, d * D + ic * 128: d * D + (ic + 1) * 128],
                                      gfm[:, d * CAP + so: d * CAP + so + sw],
                                      start=(d == 0), stop=(d == 15),
                                      inc=(pe, 1) if d == 15 else None)
                            mm_done = pe.n
                            SC.wait(pe, mm_done)
                            SC.op("activation", gusp[:, ic * CAP + so: ic * CAP + so + sw],
                                  bk_["ps"][:, 0:sw], AF.Silu)
                            bk_["ev"] = (ac, ac.n)
                    pe_b4u = pe.n
                    for c in range(16):
                        GP.wait(pe, pe_b4u)
                        GP.dma(Wbuf[:, c * D:(c + 1) * D], WuB[e, c * 128:(c + 1) * 128, :], s_w)
                    wu_rdy = s_w.n
                    gidx = 0
                    for so, sw in SEGS:
                        for ic in range(16):
                            bk_ = gsel[gidx % 2]
                            gidx += 1
                            if gidx == 1:
                                TE.wait(s_w, wu_rdy)
                            TE.wait(bk_["ev"][0], bk_["ev"][1])
                            for d in range(16):
                                TE.op("matmul", bk_["ps"][:, 0:sw],
                                      Wbuf[:, d * D + ic * 128: d * D + (ic + 1) * 128],
                                      gfm[:, d * CAP + so: d * CAP + so + sw],
                                      start=(d == 0), stop=(d == 15),
                                      inc=(pe, 1) if d == 15 else None)
                            mm_done = pe.n
                            VE.wait(pe, mm_done)
                            VE.op("scalar_tensor_tensor",
                                  gusp[:, ic * CAP + so: ic * CAP + so + sw],
                                  gusp[:, ic * CAP + so: ic * CAP + so + sw],
                                  7.0, bk_["ps"][:, 0:sw], ALU.min, ALU.mult)
                            bk_["ev"] = (ve, ve.n)
                    pe_after_u = pe.n
                    first_d = True
                    for hg in range(4):
                        qsl = hg % 2
                        for c in range(16):
                            GP.wait(pe, wdq_user[qsl])
                            GP.dma(wdq[qsl][:, c * 512:(c + 1) * 512],
                                   WdB[e, c * 128:(c + 1) * 128, hg * 512:(hg + 1) * 512],
                                   s_wd[qsl])
                        wq_rdy[qsl] = s_wd[qsl].n
                        if hg == 0 and e + 1 < NEXP:
                            for c in range(16):
                                GP.wait(pe, pe_after_u)
                                GP.dma(Wbuf[:, c * D:(c + 1) * D],
                                       WgB[e + 1, c * 128:(c + 1) * 128, :], s_w)
                            wg_rdy = s_w.n
                        for ct in range(NCT):
                            bk_ = gsel[ct % 2]
                            hslot = ct % 2
                            sas = s_sa[hg * 4 + (ct % 2) * 2 + es]
                            TE.wait(bk_["ev"][0], bk_["ev"][1])
                            TE.wait(s_wd[qsl], wq_rdy[qsl])
                            if first_d:
                                TE.wait(ve)
                                first_d = False
                            for ic in range(16):
                                TE.op("matmul", bk_["ps"][:],
                                      gusp[:, ic * CAP + ct * 128: ic * CAP + (ct + 1) * 128],
                                      wdq[qsl][:, ic * 512:(ic + 1) * 512],
                                      start=(ic == 0), stop=(ic == 15),
                                      inc=(pe, 1) if ic == 15 else None)
                            mm_done = pe.n
                            if ct >= NCT - 2:
                                wdq_user[qsl] = mm_done
                            VE.wait(pe, mm_done)
                            if hsc_guard[hslot] is not None:
                                VE.wait(hsc_guard[hslot][0], hsc_guard[hslot][1])
                            VE.op("tensor_scalar_mul", hsc[:, hslot * 512:(hslot + 1) * 512],
                                  bk_["ps"][:], gcol[:, ct:ct + 1])
                            bk_["ev"] = (ve, ve.n)
                            gcol_user2[es] = ve.n
                            GP.wait(ve, bk_["ev"][1])
                            if first_accum[0]:
                                GP.wait(s_o)
                                first_accum[0] = False
                            if ct < 2:
                                for p_ in range(2):
                                    for ep_ in range(2):
                                        GP.wait(s_sa[hg * 4 + p_ * 2 + ep_])
                            GP.op("indirect_dma_start",
                                  out=out[:],
                                  out_offset=bass.IndirectOffsetOnAxis(ap=idxg[:, ct:ct + 1], axis=0),
                                  in_=hsc[:, hslot * 512:(hslot + 1) * 512],
                                  in_offset=None,
                                  element_offset=hg * 512,
                                  compute_op=ALU.add,
                                  inc=(sas, 16), chain=False)
                            hsc_guard[hslot] = (sas, sas.n)

        for d, src in dbg_outs:
            SY.wait(s_kv)
            SY.wait(s_ct)
            SY.dma(d, src, s_o)
        for s_ in s_sa:
            SY.wait(s_)
        SY.wait(s_o)
        SY.wait(s_kv)
        SY.wait(s_h2)
        SY.wait(s_tr)
        SY.wait(s_ct)

        with contextlib.ExitStack() as ss:
            for s in P.sems.values():
                s.h = ss.enter_context(nc.semaphore(s.name))
            with nc.Block() as block:
                @block.sync
                def _(e):
                    P.engs["sync"].replay(e)

                @block.tensor
                def _(e):
                    P.engs["tensor"].replay(e)

                @block.vector
                def _(e):
                    P.engs["vector"].replay(e)

                @block.scalar
                def _(e):
                    P.engs["scalar"].replay(e)

                @block.gpsimd
                def _(e):
                    P.engs["gpsimd"].replay(e)

    return nc



def _build_masks(cp):
    m = np.zeros((NCHUNK, 16, 128, QC), dtype=np.float32)
    for k in range(NCHUNK):
        span = (k + 1) * 2048
        q0 = k * 2048 + cp * QC
        kv = (span - 2048) + np.arange(16 * 128).reshape(16, 128)
        q = q0 + np.arange(QC)
        m[k] = (kv[:, :, None] <= q[None, None, :]).astype(np.float32)
    return m.astype(BF)


_NC_CACHE = {}


def _get_nc(key=("p1", "p2", "p3", "p4"), dbg=()):
    ck = (key, dbg)
    if ck not in _NC_CACHE:
        _NC_CACHE[ck] = build_nc(phases=key, dbg=dbg)
    return _NC_CACHE[ck]


def make_in_maps(x, Wq, bq, Wk, bk, Wv, bv, Wo, bo, Wr, Wg, Wu, Wd):
    x = np.asarray(x, dtype=np.float32)
    shared = dict(
        WqT=np.ascontiguousarray(np.asarray(Wq, np.float32).T),
        WkT=np.ascontiguousarray(np.asarray(Wk, np.float32).T),
        WvT=np.ascontiguousarray(np.asarray(Wv, np.float32).T),
        WoT=np.ascontiguousarray(np.asarray(Wo, np.float32).T),
        bq=np.asarray(bq, np.float32), bk=np.asarray(bk, np.float32),
        bv=np.asarray(bv, np.float32), bo=np.asarray(bo, np.float32),
        Wr=np.ascontiguousarray(np.asarray(Wr, np.float32)),
        Wg=np.ascontiguousarray(np.asarray(Wg, np.float32)),
        Wu=np.ascontiguousarray(np.asarray(Wu, np.float32)),
        Wd=np.ascontiguousarray(np.asarray(Wd, np.float32)),
    )
    in_maps = []
    for c in range(NCORE):
        b, cp = c // 4, c % 4
        rows = [x[b, k * 2048 + cp * QC: k * 2048 + (cp + 1) * QC] for k in range(NCHUNK)]
        in_maps.append(dict(shared,
                            xq=np.ascontiguousarray(np.concatenate(rows, 0)),
                            xkv=np.ascontiguousarray(x[b]),
                            masks=np.ascontiguousarray(_build_masks(cp)[0])))
    return in_maps


def assemble_out(results):
    outf = np.empty((NB, S, D), dtype=np.float32)
    for c in range(NCORE):
        b, cp = c // 4, c % 4
        o = results[c]["out"]
        for k in range(NCHUNK):
            outf[b, k * 2048 + cp * QC: k * 2048 + (cp + 1) * QC] = o[k * QC:(k + 1) * QC]
    return outf


def kernel(L0_attn_x, L0_attn_Wq, L0_attn_bq, L0_attn_Wk, L0_attn_bk,
           L0_attn_Wv, L0_attn_bv, L0_attn_Wo, L0_attn_bo,
           L0_ffn_Wr, L0_ffn_Wg, L0_ffn_Wu, L0_ffn_Wd):
    in_maps = make_in_maps(L0_attn_x, L0_attn_Wq, L0_attn_bq, L0_attn_Wk,
                           L0_attn_bk, L0_attn_Wv, L0_attn_bv, L0_attn_Wo,
                           L0_attn_bo, L0_ffn_Wr, L0_ffn_Wg, L0_ffn_Wu, L0_ffn_Wd)
    nc = _get_nc()
    res = run_bass_kernel_spmd(nc, in_maps, core_ids=list(range(NCORE)))
    return assemble_out(res.results)


# version 24
# speedup vs baseline: 1.0542x; 1.0181x over previous
import contextlib
import numpy as np
import ml_dtypes
import concourse.bass as bass
import concourse.mybir as mybir
from concourse.bass_utils import run_bass_kernel_spmd

F32 = mybir.dt.float32
BF16 = mybir.dt.bfloat16
AF = mybir.ActivationFunctionType
ALU = mybir.AluOpType
AX = mybir.AxisListType
BF = ml_dtypes.bfloat16

D = 2048
DK = 128
H = 16
S = 8192
T = 2048
NB = 2
NCORE = 8
NCHUNK = 4
QC = 512
EPS = 1e-5
NEXP = 8
CAP = 1024
SEGS = [(o, min(512, CAP - o)) for o in range(0, CAP, 512)]
MOE = "sparse"
SCALE = float(1.0 / np.sqrt(DK))

ENGINES = ("sync", "tensor", "vector", "scalar", "gpsimd")


class Sem:
    __slots__ = ("name", "h", "n")

    def __init__(self, name):
        self.name, self.h, self.n = name, None, 0


class Eng:
    def __init__(self, prog, name, self_sem):
        self.prog, self.name, self.self_sem = prog, name, self_sem
        self.q = []

    def wait(self, sem, val=None):
        v = sem.n if val is None else val
        if v > 0:
            self.q.append(("w", sem, v))

    def op(self, opname, *args, inc=None, chain=True, **kw):
        incs = []
        if chain and self.self_sem is not None:
            self.wait(self.self_sem)
            self.self_sem.n += 1
            incs.append((self.self_sem, 1))
        if inc is not None:
            sem, amt = inc
            sem.n += amt
            incs.append((sem, amt))
        self.q.append(("o", opname, args, kw, incs))

    def raw(self, fn):
        self.q.append(("c", fn))

    def dma(self, out, in_, sem, **kw):
        sem.n += 16
        self.q.append(("o", "dma_start", (), dict(out=out, in_=in_, **kw),
                       [(sem, 16)]))
        return sem.n

    def replay(self, eng):
        for it in self.q:
            if it[0] == "w":
                eng.wait_ge(it[1].h, it[2])
            elif it[0] == "c":
                it[1](eng)
            else:
                _, opname, args, kw, incs = it
                kw = {k: (v() if callable(v) and getattr(v, "_late", False) else v)
                      for k, v in kw.items()}
                ins = getattr(eng, opname)(*args, **kw)
                for sem, amt in incs:
                    ins.then_inc(sem.h, amt)


class Prog:
    def __init__(self):
        self.sems = {}
        self.engs = {}
        for e in ENGINES:
            ss = self._sem("chain_" + e) if e in ("vector", "scalar", "gpsimd") else None
            self.engs[e] = Eng(self, e, ss)

    def _sem(self, name):
        if name not in self.sems:
            self.sems[name] = Sem(name)
        return self.sems[name]


def build_nc(phases=("p1", "p2", "p3", "p4"), dbg=()):
    nc = bass.Bass("TRN2", target_bir_lowering=False, debug=False)
    ap = lambda t: t.ap()

    xq = ap(nc.dram_tensor("xq", [T, D], F32, kind="ExternalInput"))
    xkv = ap(nc.dram_tensor("xkv", [S, D], F32, kind="ExternalInput"))
    masks = ap(nc.dram_tensor("masks", [16, 128, QC], BF16, kind="ExternalInput"))
    WqT = ap(nc.dram_tensor("WqT", [D, D], F32, kind="ExternalInput"))
    WkT = ap(nc.dram_tensor("WkT", [D, D], F32, kind="ExternalInput"))
    WvT = ap(nc.dram_tensor("WvT", [D, D], F32, kind="ExternalInput"))
    WoT = ap(nc.dram_tensor("WoT", [D, D], F32, kind="ExternalInput"))
    bq = ap(nc.dram_tensor("bq", [D], F32, kind="ExternalInput")).rearrange("(a b) -> a b", b=1)
    bk = ap(nc.dram_tensor("bk", [D], F32, kind="ExternalInput")).rearrange("(a b) -> a b", b=1)
    bv = ap(nc.dram_tensor("bv", [D], F32, kind="ExternalInput")).rearrange("(a b) -> a b", a=1)
    bo = ap(nc.dram_tensor("bo", [D], F32, kind="ExternalInput")).rearrange("(a b) -> a b", a=1)
    Wr = ap(nc.dram_tensor("Wr", [D, NEXP], F32, kind="ExternalInput"))
    Wg = ap(nc.dram_tensor("Wg", [NEXP, D, D], F32, kind="ExternalInput"))
    Wu = ap(nc.dram_tensor("Wu", [NEXP, D, D], F32, kind="ExternalInput"))
    Wd = ap(nc.dram_tensor("Wd", [NEXP, D, D], F32, kind="ExternalInput"))
    out = ap(nc.dram_tensor("out", [T + 128, D], F32, kind="ExternalOutput"))

    WgB = ap(nc.dram_tensor("WgB", [NEXP, D, D], BF16))
    WuB = ap(nc.dram_tensor("WuB", [NEXP, D, D], BF16))
    WdB = ap(nc.dram_tensor("WdB", [NEXP, D, D], BF16))
    H2TM = ap(nc.dram_tensor("H2TM", [T + 128, D], BF16))
    GATES_D = ap(nc.dram_tensor("GATES_D", [T + 128, NEXP], F32))
    IDXTOK = ap(nc.dram_tensor("IDXTOK", [NEXP * CAP + 128, 1], mybir.dt.int32))
    Kscr = ap(nc.dram_tensor("Kscr", [H, DK, S], BF16))
    Vscr = ap(nc.dram_tensor("Vscr", [H, 128, S // 128, DK], BF16))
    Qscr = ap(nc.dram_tensor("Qscr", [H, DK, T], BF16))
    CTX = ap(nc.dram_tensor("CTXscr", [H, DK, T], BF16))
    LN2F = ap(nc.dram_tensor("LN2F", [D // 128, 128, T], BF16))

    dbg_outs = []
    for name, src, shape in (
        ("kscr", Kscr, [H, DK, S]), ("vscr", Vscr, [H, 128, S // 128, DK]),
        ("qscr", Qscr, [H, DK, T]), ("ctx", CTX, [H, DK, T]),
        ("ln2f", LN2F, [D // 128, 128, T]),
    ):
        if name in dbg:
            d = ap(nc.dram_tensor("d_" + name, shape, BF16, kind="ExternalOutput"))
            dbg_outs.append((d, src))

    identc = ap(nc.inline_tensor(np.eye(128, dtype=BF), name="identc"))
    ltc = ap(nc.inline_tensor(np.tril(np.ones((128, 128), dtype=np.float32), -1).astype(BF), name="ltc"))
    tokidc = ap(nc.inline_tensor(
        (np.arange(128)[:, None] + 128 * np.arange(16)[None, :]).astype(np.int32), name="tokidc"))
    offsc = ap(nc.inline_tensor((CAP * np.arange(NEXP, dtype=np.float32))[None, :].copy(), name="offsc"))
    idxinitc = ap(nc.inline_tensor(
        (T + (np.arange(NEXP * CAP + 128) % 128)).astype(np.int32)[:, None].copy(), name="idxinitc"))
    bigpc = ap(nc.inline_tensor(
        (NEXP * CAP + np.arange(128, dtype=np.float32))[:, None].copy(), name="bigpc"))
    capendc = ap(nc.inline_tensor(
        np.broadcast_to((CAP * (np.arange(NEXP, dtype=np.float32) + 1))[None, :], (128, NEXP)).copy(),
        name="capendc"))

    P = Prog()
    SY, TE, VE, SC, GP = (P.engs[e] for e in ENGINES)
    sm = P._sem
    s_w, s_kv, s_at, s_ct, s_o, s_ms = (
        sm("s_w"), sm("s_kv"), sm("s_at"), sm("s_ct"), sm("s_o"), sm("s_ms"))
    s_x01 = [sm("s_x0"), sm("s_x1")]
    s_ev = [sm("s_ev0"), sm("s_ev1")]
    s_e2 = [sm("s_e20"), sm("s_e21")]
    s_h = [sm("s_h0"), sm("s_h1")]
    s_mk = sm("s_mk")
    s_h2 = sm("s_h2")
    s_tr = sm("s_tr")
    s_ic = sm("s_ic")
    s_sc01 = [sm("s_sc0"), sm("s_sc1")]
    s_idx = sm("s_idx")
    s_g = [sm("s_g0"), sm("s_g1")]
    s_gc = [sm("s_gc0"), sm("s_gc1")]
    s_wd = [sm("s_wd0"), sm("s_wd1")]
    s_cl = [sm("s_cl0"), sm("s_cl1")]
    s_cs = [sm("s_cs0"), sm("s_cs1")]
    s_sa = [sm(f"s_sa{h_}{p_}{ep_}") for h_ in range(4) for p_ in range(2)
            for ep_ in range(2)]
    pe = sm("pe")
    ve = sm("chain_vector")
    ac = sm("chain_scalar")
    gp = sm("chain_gpsimd")

    es = contextlib.ExitStack()

    def sb(stack, name, shape, dtype):
        return stack.enter_context(nc.sbuf_tensor(name, shape, dtype))

    with es:
        ident = sb(es, "ident", [128, 128], BF16)
        ones_mat = sb(es, "ones_mat", [128, 128], BF16)
        ones_col = sb(es, "ones_col", [128, 1], BF16)
        ones_rowb = sb(es, "ones_rowb", [1, 128], BF16)
        ones_rowf = sb(es, "ones_rowf", [1, 128], F32)
        bq_sb = sb(es, "bq_sb", [128, 16], F32)
        bk_sb = sb(es, "bk_sb", [128, 16], F32)
        bqs_sb = sb(es, "bqs_sb", [128, 16], F32)
        bv_row = sb(es, "bv_row", [1, D], BF16)
        wr_sb = sb(es, "wr_sb", [128, 16 * NEXP], BF16)
        Wbuf = sb(es, "Wbuf", [128, 16 * D], BF16)
        evb = sb(es, "evb", [128, 2 * 512], BF16)
        gates = sb(es, "gates", [128, 16 * NEXP], F32)
        colA = sb(es, "colA", [128, 1], F32)
        colB = sb(es, "colB", [128, 1], F32)
        colC = sb(es, "colC", [128, 1], F32)
        colD = sb(es, "colD", [128, 1], F32)
        scol = sb(es, "scol", [128, 1], F32)

        ps_a = es.enter_context(nc.psum_tensor("ps_a", [128, 512], F32))
        ps_b = es.enter_context(nc.psum_tensor("ps_b", [128, 512], F32))
        ps_c = es.enter_context(nc.psum_tensor("ps_c", [128, 512], F32))
        ps_d = es.enter_context(nc.psum_tensor("ps_d", [128, 512], F32))
        ps_e = es.enter_context(nc.psum_tensor("ps_e", [128, 512], F32))
        ps_f = es.enter_context(nc.psum_tensor("ps_f", [128, 512], F32))
        ps_t1 = es.enter_context(nc.psum_tensor("ps_t1", [128, 1024], BF16))
        ps_t2 = es.enter_context(nc.psum_tensor("ps_t2", [128, 1024], BF16))

        SY.dma(ident[:], identc[:], s_ms)
        VE.op("memset", ones_mat[:], 1.0)
        VE.op("memset", ones_col[:], 1.0)
        VE.op("memset", ones_rowb[:], 1.0)
        VE.op("memset", ones_rowf[:], 1.0)
        for c in range(16):
            SY.dma(bq_sb[:, c:c + 1], bq[c * 128:(c + 1) * 128, :], s_ms)
            SY.dma(bk_sb[:, c:c + 1], bk[c * 128:(c + 1) * 128, :], s_ms)
            GP.dma(wr_sb[:, c * NEXP:(c + 1) * NEXP], Wr[c * 128:(c + 1) * 128, :], s_w)
        GP.dma(bv_row[:], bv[:], s_w)
        VE.wait(s_ms)
        VE.wait(s_w)
        VE.op("tensor_scalar_mul", bqs_sb[:], bq_sb[:], SCALE)
        VE.op("memset", scol[:], SCALE)

        if "p1" in phases:
            p1 = contextlib.ExitStack()
            with p1:
                Wbuf2 = sb(p1, "Wbuf2", [128, 16 * D], BF16)
                evb2 = sb(p1, "evb2", [128, 2 * 512], BF16)
                sqb = sb(p1, "sqb", [128, D // 2], F32)
                xg = sb(p1, "xg", [128, 2 * D], F32)
                lnt = sb(p1, "lnt", [128, 2 * D], BF16)
                lnf2 = [sb(p1, "lnf0", [128, 16 * 512], BF16),
                        sb(p1, "lnf1", [128, 16 * 512], BF16)]

                for c in range(16):
                    GP.dma(Wbuf[:, c * D:(c + 1) * D], WkT[c * 128:(c + 1) * 128, :], s_w)
                    GP.dma(Wbuf2[:, c * D:(c + 1) * D], WvT[c * 128:(c + 1) * 128, :], s_w)
                w_ready_kv = s_w.n

                ev_slot_dma = [0, 0]
                ev2_slot_dma = [0, 0]
                xg_slot_user = [0, 0]
                xg_slot_ac = [0, 0]

                def ln_tile_to_fm(src_dram, row0, tile_idx4, xslot):
                    SY.wait(ve, xg_slot_user[xslot])
                    SY.wait(ac, xg_slot_ac[xslot])
                    sx = s_x01[xslot]
                    SY.dma(xg[:, xslot * D:(xslot + 1) * D], src_dram[row0:row0 + 128, :], sx)
                    xv = sx.n
                    xt_ = xg[:, xslot * D:(xslot + 1) * D]
                    VE.wait(sx, xv)
                    VE.op("reduce_sum", colA[:], xt_, axis=AX.X)
                    SC.wait(sx, xv)
                    SC.wait(ve)
                    SC.op("activation", sqb[:], xt_[:, 0:D // 2], AF.Square)
                    VE.wait(ac)
                    VE.op("reduce_sum", colB[:], sqb[:], axis=AX.X)
                    SC.wait(ve)
                    SC.op("activation", sqb[:], xt_[:, D // 2:D], AF.Square)
                    VE.wait(ac)
                    VE.op("reduce_sum", colC[:], sqb[:], axis=AX.X)
                    VE.op("tensor_add", colB[:], colB[:], colC[:])
                    VE.op("tensor_scalar_mul", colA[:], colA[:], 1.0 / D)
                    VE.op("tensor_scalar_mul", colB[:], colB[:], 1.0 / D)
                    VE.op("tensor_mul", colC[:], colA[:], colA[:])
                    VE.op("tensor_scalar", colB[:], colB[:], colC[:, 0:1], EPS,
                          ALU.subtract, ALU.add)
                    SC.wait(ve)
                    SC.op("activation", colC[:], colB[:], AF.Ln)
                    SC.op("activation", colD[:], colC[:], AF.Exp, scale=-0.5)
                    VE.wait(ac)
                    lslot = tile_idx4 % 2
                    VE.wait(pe, lnt_tp[lslot])
                    VE.op("tensor_scalar", lnt[:, lslot * D:(lslot + 1) * D], xt_,
                          colA[:, 0:1], colD[:, 0:1], ALU.subtract, ALU.mult)
                    xg_slot_user[xslot] = ve.n
                    xg_slot_ac[xslot] = ac.n
                    lnf = lnf2[gpar[0]]
                    ts_done = ve.n
                    TE.wait(ve, ts_done)
                    TE.wait(pe, lnf_used[gpar[0]])
                    for d in range(16):
                        tb = [ps_t1, ps_t2][d % 2]
                        TE.wait(ac, tp_ac[d % 2])
                        TE.op("transpose", tb[:, 0:128],
                              lnt[:, lslot * D + d * 128: lslot * D + (d + 1) * 128],
                              ident[:], inc=(pe, 1))
                        tp_pe[d % 2] = pe.n
                        SC.wait(pe, tp_pe[d % 2])
                        SC.op("activation",
                              lnf[:, d * 512 + tile_idx4 * 128: d * 512 + (tile_idx4 + 1) * 128],
                              tb[:, 0:128], AF.Identity)
                        tp_ac[d % 2] = ac.n
                    lnt_tp[lslot] = pe.n

                lnt_tp = [0, 0]
                tp_ac = [0, 0]
                tp_pe = [0, 0]
                gpar = [0]
                lnf_used = [0, 0]
                grp_tp = [0, 0]

                def group_fm(src_dram, g0):
                    for i in range(4):
                        ln_tile_to_fm(src_dram, g0 + i * 128, i, i % 2)
                    grp_tp[gpar[0]] = ac.n

                def proj_fm(Wb, bias_col, scale, dst, g, bank_sel, nchunks=16):
                    lnf = lnf2[gpar[0]]
                    for e in range(nchunks):
                        bank = bank_sel[e % 2]
                        TE.wait(ac, grp_tp[gpar[0]])
                        TE.wait(ac, bank["ac"])
                        for d in range(16):
                            TE.op("matmul", bank["ps"][:],
                                  Wb[:, d * D + e * 128: d * D + (e + 1) * 128],
                                  lnf[:, d * 512:(d + 1) * 512],
                                  start=(d == 0), stop=(d == 15),
                                  inc=(pe, 1) if d == 15 else None)
                        bank["pe"] = pe.n
                        slot = e % 2
                        SC.wait(pe, bank["pe"])
                        SC.wait(s_ev[slot], ev_slot_dma[slot])
                        SC.op("activation", evb[:, slot * 512:(slot + 1) * 512],
                              bank["ps"][:], AF.Identity,
                              bias=bias_col(e), scale=scale)
                        bank["ac"] = ac.n
                        SY.wait(ac, bank["ac"])
                        ev_slot_dma[slot] = SY.dma(dst(e, g), evb[:, slot * 512:(slot + 1) * 512], s_ev[slot])

                bankA = {"ps": ps_a, "pe": 0, "ac": 0}
                bankB = {"ps": ps_b, "pe": 0, "ac": 0}
                bankC = {"ps": ps_c, "pe": 0, "ve": 0}
                bankD = {"ps": ps_d, "pe": 0, "ve": 0}

                for g in range(16):
                    group_fm(xkv, g * 512)
                    if g == 0:
                        TE.wait(s_w, w_ready_kv)
                    lnf = lnf2[gpar[0]]
                    proj_fm(Wbuf, lambda e: bk_sb[:, e:e + 1], 1.0,
                            lambda e, gg=g: Kscr[e, :, gg * 512:(gg + 1) * 512],
                            g, [bankA, bankB])
                    vsel = [bankC, bankD]
                    vidx = 0
                    for eg in range(4):
                        for i in range(4):
                            bank = vsel[vidx % 2]
                            slot = vidx % 2
                            vidx += 1
                            TE.wait(ve, bank["ve"])
                            for d in range(16):
                                TE.op("matmul", bank["ps"][:],
                                      lnf[:, d * 512 + i * 128: d * 512 + (i + 1) * 128],
                                      Wbuf2[:, d * D + eg * 512: d * D + (eg + 1) * 512],
                                      start=(d == 0), stop=False)
                            TE.op("matmul", bank["ps"][:], ones_rowb[:],
                                  bv_row[:, eg * 512:(eg + 1) * 512],
                                  start=False, stop=True, inc=(pe, 1))
                            bank["pe"] = pe.n
                            VE.wait(pe, bank["pe"])
                            VE.wait(s_e2[slot], ev2_slot_dma[slot])
                            VE.op("tensor_copy", evb2[:, slot * 512:(slot + 1) * 512], bank["ps"][:])
                            bank["ve"] = ve.n
                            SY.wait(ve, bank["ve"])
                            for hh in range(4):
                                ev2_slot_dma[slot] = SY.dma(
                                    Vscr[eg * 4 + hh, :, g * 4 + i, :],
                                    evb2[:, slot * 512 + hh * 128: slot * 512 + (hh + 1) * 128],
                                    s_e2[slot])
                    lnf_used[gpar[0]] = pe.n
                    gpar[0] ^= 1

                pe_b4 = pe.n
                for c in range(16):
                    GP.wait(pe, pe_b4)
                    GP.dma(Wbuf[:, c * D:(c + 1) * D], WqT[c * 128:(c + 1) * 128, :], s_w)
                w_ready_q = s_w.n
                for g in range(4):
                    group_fm(xq, g * 512)
                    if g == 0:
                        TE.wait(s_w, w_ready_q)
                    proj_fm(Wbuf, lambda e: bqs_sb[:, e:e + 1], scol[:, 0:1],
                            lambda e, gg=g: Qscr[e, :, gg * 512:(gg + 1) * 512],
                            g, [bankA, bankB])
                    lnf_used[gpar[0]] = pe.n
                    gpar[0] ^= 1

        if "p2" in phases:
            p2 = contextlib.ExitStack()
            with p2:
                Kh = [sb(p2, "Kh0", [128, S], BF16), sb(p2, "Kh1", [128, S], BF16)]
                Vh = [sb(p2, "Vh0", [128, S], BF16), sb(p2, "Vh1", [128, S], BF16)]
                Qh = [sb(p2, "Qh0", [128, T], BF16), sb(p2, "Qh1", [128, T], BF16)]
                msk = sb(p2, "msk", [128, 16 * QC], BF16)
                sqs = sb(p2, "sqs", [128, 2 * 2048], BF16)
                cstg = sb(p2, "cstg", [128, 2 * 2048], BF16)
                pt = sb(p2, "pt", [128, 4 * QC], BF16)
                ctxh = sb(p2, "ctxh", [128, T], BF16)
                den_row = sb(p2, "den_row", [1, 2 * QC], F32)
                rec_row = sb(p2, "rec_row", [1, 2 * QC], F32)
                recb = sb(p2, "recb", [128, QC], F32)
                cxe = sb(p2, "cxe", [128, 2 * QC], F32)
                negc = sb(p2, "negc", [128, 1], F32)
                kq2 = sb(p2, "kq2", [128, 1], F32)

                SY.wait(s_ev[0])
                SY.wait(s_ev[1])
                SY.wait(s_e2[0])
                SY.wait(s_e2[1])
                head_mm_done = [0, 0]
                ld_done = [0, 0]
                SY.dma(msk[:].rearrange("p (j q) -> p j q", q=QC),
                       masks.rearrange("j p q -> p j q"), s_mk)
                mv = s_mk.n
                SY.dma(Kh[0][:], Kscr[0], s_h[0])
                SY.dma(Vh[0][:].rearrange("p (t j) -> p t j", j=DK), Vscr[0], s_h[0])
                SY.dma(Qh[0][:], Qscr[0], s_h[0])
                ld_done[0] = s_h[0].n
                ctx_dma = [0]
                st2 = {"ctx_free": 0, "den_free": 0,
                       "recb_user": 0, "rec_user_pe": 0, "cxe_user": [0, 0]}
                pend = []

                cchunks = []
                for e_ in range(NEXP):
                    for src_, dst_ in ((Wg, WgB), (Wu, WuB), (Wd, WdB)):
                        for c_ in range(16):
                            cchunks.append((src_, dst_, e_, c_))
                cv = {"load_i": 0, "store_i": 0, "cl_v": [0, 0], "cs_g": [0, 0]}
                p1_quiesce = (pe.n, ac.n, ve.n)

                def emit_conv(n):
                    for _ in range(n):
                        j = cv["store_i"]
                        if j >= len(cchunks):
                            return
                        while cv["load_i"] <= min(j + 1, len(cchunks) - 1):
                            li = cv["load_i"]
                            slot = li % 2
                            src_, dst_, e_, c_ = cchunks[li]
                            if cv["load_i"] == 0:
                                GP.wait(pe, p1_quiesce[0])
                                GP.wait(ac, p1_quiesce[1])
                                GP.wait(ve, p1_quiesce[2])
                                GP.wait(s_ev[0])
                                GP.wait(s_ev[1])
                                GP.wait(s_e2[0])
                                GP.wait(s_e2[1])
                                GP.wait(s_x01[0])
                                GP.wait(s_x01[1])
                            GP.wait(s_cs[slot], cv["cs_g"][slot])
                            GP.dma(cstg[:, slot * 2048:(slot + 1) * 2048],
                                   src_[e_, c_ * 128:(c_ + 1) * 128, :], s_cl[slot])
                            cv["cl_v"][slot] = s_cl[slot].n
                            cv["load_i"] += 1
                        slot = j % 2
                        src_, dst_, e_, c_ = cchunks[j]
                        GP.wait(s_cl[slot], cv["cl_v"][slot])
                        GP.dma(dst_[e_, c_ * 128:(c_ + 1) * 128, :],
                               cstg[:, slot * 2048:(slot + 1) * 2048], s_cs[slot])
                        cv["cs_g"][slot] = s_cs[slot].n
                        cv["store_i"] += 1

                for h in range(H):
                    sl = h % 2
                    Khc, Vhc, Qhc = Kh[sl], Vh[sl], Qh[sl]
                    emit_conv(28)
                    at_ready = ld_done[sl]
                    if h + 1 < H:
                        nsl = (h + 1) % 2
                        SY.wait(pe, head_mm_done[nsl])
                        SY.dma(Kh[nsl][:], Kscr[h + 1], s_h[nsl])
                        SY.dma(Vh[nsl][:].rearrange("p (t j) -> p t j", j=DK), Vscr[h + 1], s_h[nsl])
                        SY.dma(Qh[nsl][:], Qscr[h + 1], s_h[nsl])
                        ld_done[nsl] = s_h[nsl].n

                    SC.wait(s_h[sl], at_ready)
                    TE.wait(ac)
                    TE.wait(ve)
                    nrm_ve = [0, 0]
                    sq_pe = [0, 0]
                    sqi = 0
                    for which, buf, width in (("q", Qhc, T), ("k", Khc, S)):
                        first = True
                        for idx in range(width // 2048):
                            sslot = sqi % 2
                            sqi += 1
                            SC.wait(pe, sq_pe[sslot])
                            SC.op("activation", sqs[:, sslot * 2048:(sslot + 1) * 2048],
                                  buf[:, idx * 2048:(idx + 1) * 2048], AF.Square)
                            sq_ac = ac.n
                            for q4 in range(4):
                                bsel = q4 % 2
                                bank = [ps_a, ps_b][bsel]
                                TE.wait(ac, sq_ac)
                                TE.wait(ve, nrm_ve[bsel])
                                TE.op("matmul", bank[:], ones_mat[:],
                                      sqs[:, sslot * 2048 + q4 * 512: sslot * 2048 + (q4 + 1) * 512],
                                      start=True, stop=True, inc=(pe, 1))
                                mmv = pe.n
                                VE.wait(pe, mmv)
                                VE.op("reduce_max", colA[:], bank[:], axis=AX.X)
                                if first:
                                    VE.op("tensor_copy", colB[:], colA[:])
                                    first = False
                                else:
                                    VE.op("tensor_max", colB[:], colB[:], colA[:])
                                nrm_ve[bsel] = ve.n
                            sq_pe[sslot] = pe.n
                        if which == "q":
                            VE.op("tensor_copy", colC[:], colB[:])
                    VE.op("tensor_mul", kq2[:], colC[:], colB[:])
                    SC.wait(ve)
                    SC.op("activation", colD[:], kq2[:], AF.Ln)
                    SC.op("activation", negc[:], colD[:], AF.Exp, scale=0.5)
                    VE.wait(ac)
                    VE.op("tensor_scalar_mul", negc[:], negc[:], float(-1.02 * SCALE))
                    negc_ve = ve.n
                    TE.wait(ve, negc_ve)

                    def finish_norm(pd):
                        k0, slot, rec_ready = pd
                        TE.wait(ve, rec_ready)
                        TE.wait(ve, st2["recb_user"])
                        TE.op("matmul", ps_e[:], ones_rowf[:], rec_row[:, slot * QC:(slot + 1) * QC],
                              start=True, stop=True, inc=(pe, 1))
                        st2["rec_user_pe"] = pe.n
                        VE.wait(pe, st2["rec_user_pe"])
                        VE.op("tensor_copy", recb[:], ps_e[:])
                        VE.wait(s_ct, ctx_dma[0])
                        VE.op("tensor_mul", ctxh[:, k0 * QC:(k0 + 1) * QC],
                              cxe[:, slot * QC:(slot + 1) * QC], recb[:])
                        st2["recb_user"] = ve.n
                        st2["cxe_user"][slot] = ve.n

                    for k in (3, 2, 1, 0):
                        ntile = 16 * (k + 1)
                        sbank = [{"ps": ps_a, "ac": 0}, {"ps": ps_b, "ac": 0}, {"ps": ps_f, "ac": 0}]
                        pt_free = [0, 0, 0, 0]
                        exp_ac = [0] * ntile
                        mask_ve = [0] * ntile
                        sj_pe = [0] * ntile
                        for step in range(ntile + 1):
                            if step < ntile:
                                j = step
                                bk_ = sbank[j % 3]
                                TE.wait(ac, bk_["ac"])
                                TE.op("matmul", bk_["ps"][:],
                                      Khc[:, j * 128:(j + 1) * 128],
                                      Qhc[:, k * QC:(k + 1) * QC],
                                      start=True, stop=True, inc=(pe, 1))
                                sj_pe[j] = pe.n
                                ptt = pt[:, (j % 4) * QC:(j % 4 + 1) * QC]
                                SC.wait(pe, max(sj_pe[j], pt_free[j % 4]))
                                SC.wait(ve, negc_ve)
                                SC.op("activation", ptt, bk_["ps"][:], AF.Exp, bias=negc[:, 0:1])
                                exp_ac[j] = ac.n
                                bk_["ac"] = ac.n
                                if j >= ntile - 16:
                                    m = j - (ntile - 16)
                                    VE.wait(s_mk, mv)
                                    VE.wait(ac, exp_ac[j])
                                    VE.op("tensor_mul", ptt, ptt, msk[:, m * QC:(m + 1) * QC])
                                    mask_ve[j] = ve.n
                            if step == 4 and len(pend) == 2:
                                finish_norm(pend.pop(0))
                            if step >= 1:
                                i = step - 1
                                ptt = pt[:, (i % 4) * QC:(i % 4 + 1) * QC]
                                TE.wait(ac, exp_ac[i])
                                if mask_ve[i]:
                                    TE.wait(ve, mask_ve[i])
                                if i == 0:
                                    TE.wait(ve, st2["ctx_free"])
                                    TE.wait(ve, st2["den_free"])
                                TE.op("matmul", ps_d[0:1, :], ones_col[:], ptt,
                                      start=(i == 0), stop=(i == ntile - 1))
                                TE.op("matmul", ps_c[:], Vhc[:, i * 128:(i + 1) * 128], ptt,
                                      start=(i == 0), stop=(i == ntile - 1), inc=(pe, 1))
                                pt_free[i % 4] = pe.n
                        head_mm_done[sl] = pe.n
                        slot = k % 2
                        VE.wait(pe)
                        VE.op("tensor_copy", cxe[:, slot * QC:(slot + 1) * QC], ps_c[:])
                        st2["ctx_free"] = ve.n
                        VE.op("tensor_copy", den_row[:, slot * QC:(slot + 1) * QC], ps_d[0:1, :])
                        st2["den_free"] = ve.n
                        VE.wait(pe, st2["rec_user_pe"])
                        VE.op("reciprocal", rec_row[:, slot * QC:(slot + 1) * QC],
                              den_row[:, slot * QC:(slot + 1) * QC])
                        pend.append((k, slot, ve.n))
                    while pend:
                        finish_norm(pend.pop(0))
                    SY.wait(ve)
                    ctx_dma[0] = SY.dma(CTX[h], ctxh[:], s_ct)

        if "p3" in phases:
            p3 = contextlib.ExitStack()
            with p3:
                ctxfull = sb(p3, "ctxfull", [128, H * T], BF16)
                bo_row = sb(p3, "bo_row", [1, D], BF16)
                sqb = sb(p3, "sqb3", [128, D // 2], F32)
                xqt = sb(p3, "xqt", [128, 2 * D], F32)
                x2t = sb(p3, "x2t", [128, D], F32)
                h2t = sb(p3, "h2t", [128, D], BF16)
                trb = sb(p3, "trb", [128, D], BF16)
                logit = sb(p3, "logit", [128, NEXP], F32)
                tcC = sb(p3, "tcC", [128, NEXP], F32)
                tcD = sb(p3, "tcD", [128, NEXP], F32)
                tcE = sb(p3, "tcE", [128, NEXP], F32)
                if MOE == "sparse":
                    lt_sb = sb(p3, "lt_sb", [128, 128], BF16)
                    tokid_sb = sb(p3, "tokid_sb", [128, 16], mybir.dt.int32)
                    offs_sb = sb(p3, "offs_sb", [1, NEXP], F32)
                    capend_sb = sb(p3, "capend_sb", [128, NEXP], F32)
                    selx = sb(p3, "selx", [128, NEXP], F32)
                    selb = sb(p3, "selb", [128, NEXP], BF16)
                    pfx = sb(p3, "pfx", [128, NEXP], F32)
                    slotf = sb(p3, "slotf", [128, NEXP], F32)
                    slot_i = sb(p3, "slot_i", [128, 2 * NEXP], mybir.dt.int32)
                    bigp_sb = sb(p3, "bigp_sb", [128, 1], F32)
                    totrow = sb(p3, "totrow", [1, NEXP], F32)
                    baserow = sb(p3, "baserow", [1, NEXP], F32)
                    boff = sb(p3, "boff", [1, NEXP], F32)

                pe_b4 = pe.n
                GP.wait(pe, pe_b4)
                GP.dma(bo_row[:], bo[:], s_w)
                for c in range(16):
                    GP.wait(pe, pe_b4)
                    GP.dma(Wbuf[:, c * D:(c + 1) * D], WoT[c * 128:(c + 1) * 128, :], s_w)
                w_ready_o = s_w.n
                trash_kv0 = [0]
                if MOE == "sparse":
                    SY.dma(lt_sb[:], ltc[:], s_ms)
                    SY.dma(tokid_sb[:], tokidc[:], s_ms)
                    SY.dma(offs_sb[:], offsc[:], s_ms)
                    SY.dma(capend_sb[:], capendc[:], s_ms)
                    SY.dma(bigp_sb[:], bigpc[:], s_ms)
                    VE.op("memset", baserow[:], 0.0)
                    VE.wait(s_ms)
                    gp_init = GP.dma(IDXTOK[:], idxinitc[:], s_ic)
                    VE.op("memset", tcC[:], 0.0)
                    VE.op("memset", h2t[:], 0.0)
                    trash_ve = ve.n
                    GP.wait(ve, trash_ve)
                    GP.dma(H2TM[T:T + 128, :], h2t[:], s_tr)
                    GP.dma(GATES_D[T:T + 128, :], tcC[:], s_tr)
                    trash_kv0[0] = s_tr.n
                    GP.wait(s_ic, gp_init)
                SY.wait(s_ct)
                for hh in range(H):
                    SY.dma(ctxfull[:, hh * T:(hh + 1) * T], CTX[hh], s_at)
                ctx_ready = s_at.n
                xq_dma = [0, 0]
                xq_user = [0, 0]
                SY.dma(xqt[:, 0:D], xq[0:128, :], s_x01[0])
                xq_dma[0] = s_x01[0].n
                ob = [{"ps": ps_a, "ve": 0}, {"ps": ps_b, "ve": 0}]
                oidx = 0
                x2_dma = [0]
                h2_tp = [0]
                tr_dma = [0]
                x2_sc = [0]
                rtr_pe = [0]
                tp_ac3 = [0, 0]
                tp_pe3 = [0, 0]
                h2tm_dma = [0]
                idx_ve = [0]
                slot_sc = [0, 0]

                for i in range(16):
                    if i + 1 < 16:
                        SY.wait(ve, xq_user[(i + 1) % 2])
                        SY.dma(xqt[:, ((i + 1) % 2) * D:((i + 1) % 2 + 1) * D],
                               xq[(i + 1) * 128:(i + 2) * 128, :], s_x01[(i + 1) % 2])
                        xq_dma[(i + 1) % 2] = s_x01[(i + 1) % 2].n
                    if i == 0:
                        TE.wait(s_w, w_ready_o)
                        TE.wait(s_at, ctx_ready)
                    for eg in range(4):
                        bk_ = ob[oidx % 2]
                        oidx += 1
                        TE.wait(ve, bk_["ve"])
                        for cd in range(16):
                            TE.op("matmul", bk_["ps"][:],
                                  ctxfull[:, cd * T + i * 128: cd * T + (i + 1) * 128],
                                  Wbuf[:, cd * D + eg * 512: cd * D + (eg + 1) * 512],
                                  start=(cd == 0), stop=False)
                        TE.op("matmul", bk_["ps"][:], ones_rowb[:],
                              bo_row[:, eg * 512:(eg + 1) * 512],
                              start=False, stop=True, inc=(pe, 1))
                        VE.wait(pe)
                        VE.wait(s_x01[i % 2], xq_dma[i % 2])
                        VE.wait(s_o, x2_dma[0])
                        VE.wait(ac, x2_sc[0])
                        VE.op("tensor_add", x2t[:, eg * 512:(eg + 1) * 512], bk_["ps"][:],
                              xqt[:, (i % 2) * D + eg * 512:(i % 2) * D + (eg + 1) * 512])
                        bk_["ve"] = ve.n
                    xq_user[i % 2] = ve.n
                    SY.wait(ve)
                    x2_dma[0] = SY.dma(out[i * 128:(i + 1) * 128, :], x2t[:], s_o)
                    VE.op("reduce_sum", colA[:], x2t[:], axis=AX.X)
                    SC.wait(ve)
                    SC.op("activation", sqb[:], x2t[:, 0:D // 2], AF.Square)
                    VE.wait(ac)
                    VE.op("reduce_sum", colB[:], sqb[:], axis=AX.X)
                    SC.wait(ve)
                    SC.op("activation", sqb[:], x2t[:, D // 2:D], AF.Square)
                    VE.wait(ac)
                    VE.op("reduce_sum", colC[:], sqb[:], axis=AX.X)
                    VE.op("tensor_add", colB[:], colB[:], colC[:])
                    VE.op("tensor_scalar_mul", colA[:], colA[:], 1.0 / D)
                    VE.op("tensor_scalar_mul", colB[:], colB[:], 1.0 / D)
                    VE.op("tensor_mul", colC[:], colA[:], colA[:])
                    VE.op("tensor_scalar", colB[:], colB[:], colC[:, 0:1], EPS,
                          ALU.subtract, ALU.add)
                    SC.wait(ve)
                    SC.op("activation", colC[:], colB[:], AF.Ln)
                    SC.op("activation", colD[:], colC[:], AF.Exp, scale=-0.5)
                    x2_sc[0] = ac.n
                    VE.wait(ac)
                    VE.wait(pe, h2_tp[0])
                    VE.wait(s_h2, h2tm_dma[0])
                    VE.wait(s_tr, trash_kv0[0])
                    VE.op("tensor_scalar", h2t[:], x2t[:], colA[:, 0:1], colD[:, 0:1],
                          ALU.subtract, ALU.mult)
                    h2_ve = ve.n
                    TE.wait(ve, h2_ve)
                    TE.wait(pe, rtr_pe[0])
                    for d in range(16):
                        tb = [ps_t1, ps_t2][d % 2]
                        TE.wait(ac, tp_ac3[d % 2])
                        TE.op("transpose", tb[:, 0:128], h2t[:, d * 128:(d + 1) * 128],
                              ident[:], inc=(pe, 1))
                        tp_pe3[d % 2] = pe.n
                        SC.wait(pe, tp_pe3[d % 2])
                        SC.op("activation", trb[:, d * 128:(d + 1) * 128],
                              tb[:, 0:128], AF.Identity)
                        tp_ac3[d % 2] = ac.n
                    h2_tp[0] = pe.n
                    trb_ac = ac.n
                    TE.wait(ac, trb_ac)
                    TE.wait(ve)
                    for d in range(16):
                        TE.op("matmul", ps_f[:, 0:NEXP],
                              trb[:, d * 128:(d + 1) * 128],
                              wr_sb[:, d * NEXP:(d + 1) * NEXP],
                              start=(d == 0), stop=(d == 15),
                              inc=(pe, 1) if d == 15 else None)
                    rtr_pe[0] = pe.n
                    VE.wait(pe)
                    VE.op("tensor_copy", logit[:], ps_f[:, 0:NEXP])
                    VE.op("reduce_max", colA[:], logit[:], axis=AX.X)
                    VE.wait(s_tr, trash_kv0[0])
                    VE.op("tensor_single_scalar", tcC[:], logit[:], colA[:, 0:1], ALU.is_equal)
                    VE.op("scalar_tensor_tensor", tcD[:], tcC[:], -1e30, logit[:], ALU.mult, ALU.add)
                    VE.op("reduce_max", colB[:], tcD[:], axis=AX.X)
                    VE.op("tensor_sub", colC[:], colA[:], colB[:])
                    SC.wait(ve)
                    SC.op("activation", colD[:], colC[:], AF.Sigmoid)
                    VE.wait(ac)
                    VE.op("tensor_scalar", colC[:], colD[:], -1.0, 1.0, ALU.mult, ALU.add)
                    VE.op("tensor_single_scalar", tcD[:], logit[:], colB[:, 0:1], ALU.is_equal)
                    VE.op("tensor_scalar", tcE[:], tcC[:], -1.0, 1.0, ALU.mult, ALU.add)
                    VE.op("tensor_mul", tcD[:], tcD[:], tcE[:])
                    VE.op("tensor_scalar_mul", tcC[:], tcC[:], colD[:, 0:1])
                    VE.op("tensor_scalar_mul", tcD[:], tcD[:], colC[:, 0:1])
                    VE.op("tensor_add", gates[:, i * NEXP:(i + 1) * NEXP], tcC[:], tcD[:])
                    if MOE == "sparse":
                        gts = gates[:, i * NEXP:(i + 1) * NEXP]
                        gates_ve = ve.n
                        SY.wait(ve, gates_ve)
                        SY.dma(GATES_D[i * 128:(i + 1) * 128, :], gts, s_kv)
                        SY.wait(ve, h2_ve)
                        h2tm_dma[0] = SY.dma(H2TM[i * 128:(i + 1) * 128, :], h2t[:], s_h2)
                        VE.op("tensor_single_scalar", selx[:], gts, 0.0, ALU.is_gt)
                        VE.op("tensor_copy", selb[:], selx[:])
                        VE.op("tensor_add", boff[:], baserow[:], offs_sb[:])
                        sel_ve = ve.n
                        TE.wait(ve, sel_ve)
                        TE.wait(ve, idx_ve[0])
                        TE.op("matmul", ps_d[:, 0:NEXP], lt_sb[:], selb[:],
                              start=True, stop=True)
                        TE.op("matmul", ps_c[0:1, 0:NEXP], ones_col[:], selb[:],
                              start=True, stop=True)
                        TE.op("matmul", ps_e[:, 0:NEXP], ones_rowf[:], boff[:],
                              start=True, stop=True, inc=(pe, 1))
                        idx_pe = pe.n
                        spar = i % 2
                        VE.wait(pe, idx_pe)
                        VE.wait(s_sc01[spar], slot_sc[spar])
                        VE.op("tensor_copy", pfx[:], ps_d[:, 0:NEXP])
                        VE.op("tensor_add", pfx[:], pfx[:], ps_e[:, 0:NEXP])
                        VE.op("tensor_tensor", slotf[:], pfx[:], capend_sb[:], ALU.is_lt)
                        VE.op("tensor_mul", slotf[:], slotf[:], selx[:])
                        VE.op("tensor_scalar", pfx[:], pfx[:], bigp_sb[:, 0:1], None, ALU.subtract)
                        VE.op("tensor_mul", pfx[:], pfx[:], slotf[:])
                        VE.op("tensor_scalar", pfx[:], pfx[:], bigp_sb[:, 0:1], None, ALU.add)
                        VE.op("tensor_copy", slot_i[:, spar * NEXP:(spar + 1) * NEXP], pfx[:])
                        VE.op("tensor_copy", totrow[:], ps_c[0:1, 0:NEXP])
                        VE.op("tensor_add", baserow[:], baserow[:], totrow[:])
                        idx_ve[0] = ve.n
                        for e in range(NEXP):
                            GP.wait(ve, idx_ve[0])
                            GP.op("indirect_dma_start",
                                  out=IDXTOK[:], 
                                  out_offset=bass.IndirectOffsetOnAxis(
                                      ap=slot_i[:, spar * NEXP + e: spar * NEXP + e + 1], axis=0),
                                  in_=tokid_sb[:, i:i + 1],
                                  in_offset=None,
                                  inc=(s_sc01[spar], 16), chain=False)
                        slot_sc[spar] = s_sc01[spar].n

        if "p4" in phases and MOE == "sparse":
            NSG = CAP // 512
            NCT = CAP // 128
            p4 = contextlib.ExitStack()
            with p4:
                gfm = sb(p4, "gfm", [128, 16 * CAP], BF16)
                gusp = sb(p4, "gusp", [128, 16 * CAP], BF16)
                wdq = [sb(p4, "wdq0", [128, 16 * 512], BF16),
                       sb(p4, "wdq1", [128, 16 * 512], BF16)]
                gbuf = sb(p4, "gbuf", [128, 2 * D], BF16)
                hsc = sb(p4, "hsc", [128, 2 * 512], F32)
                idxg2 = [sb(p4, "idxg0", [128, NCT], mybir.dt.int32),
                         sb(p4, "idxg1", [128, NCT], mybir.dt.int32)]
                gcol2 = [sb(p4, "gcol0", [128, NCT], F32),
                         sb(p4, "gcol1", [128, NCT], F32)]

                gsel = [{"ps": ps_a, "ev": (ve, 0)}, {"ps": ps_b, "ev": (ve, 0)}]
                hsc_guard = [None, None]
                wdq_user = [0, 0]
                wq_rdy = [0, 0]
                first_accum = [True]
                gbuf_tp = [0, 0]
                tp_ac4 = [0, 0]
                tp_pe4 = [0, 0]
                gcol_user2 = [0, 0]

                SY.wait(s_sc01[0])
                SY.wait(s_sc01[1])
                SY.wait(s_kv)
                SY.wait(s_h2)
                SY.wait(s_tr)
                pe_b4 = pe.n
                GP.wait(s_cs[0])
                GP.wait(s_cs[1])
                for c in range(16):
                    GP.wait(pe, pe_b4)
                    GP.dma(Wbuf[:, c * D:(c + 1) * D], WgB[0, c * 128:(c + 1) * 128, :], s_w)
                wg_rdy = s_w.n
                TE.wait(ac)
                for e in range(NEXP):
                    es = e % 2
                    idxg = idxg2[es]
                    gcol = gcol2[es]
                    SY.wait(s_g[0])
                    SY.wait(s_g[1])
                    SY.wait(s_gc[0])
                    SY.wait(s_gc[1])
                    for h_ in range(4):
                        for p_ in range(2):
                            SY.wait(s_sa[h_ * 4 + p_ * 2 + es])
                    SY.wait(ve, gcol_user2[es])
                    for ct in range(NCT):
                        SY.dma(idxg[:, ct:ct + 1], IDXTOK[e * CAP + ct * 128: e * CAP + (ct + 1) * 128, :], s_idx)
                    idx_rdy = s_idx.n
                    for ct in range(NCT):
                        gsl = ct % 2
                        GP.wait(s_idx, idx_rdy)
                        GP.wait(pe, gbuf_tp[gsl])
                        GP.op("indirect_dma_start",
                              out=gbuf[:, gsl * D:(gsl + 1) * D],
                              out_offset=None,
                              in_=H2TM[:],
                              in_offset=bass.IndirectOffsetOnAxis(ap=idxg[:, ct:ct + 1], axis=0),
                              inc=(s_g[gsl], 16), chain=False)
                        GP.op("indirect_dma_start",
                              out=gcol[:, ct:ct + 1],
                              out_offset=None,
                              in_=GATES_D[:],
                              in_offset=bass.IndirectOffsetOnAxis(ap=idxg[:, ct:ct + 1], axis=0),
                              element_offset=e,
                              inc=(s_gc[ct % 2], 16), chain=False)
                        TE.wait(s_g[gsl], s_g[gsl].n)
                        TE.wait(s_gc[ct % 2], s_gc[ct % 2].n)
                        for d in range(16):
                            tb = [ps_t1, ps_t2][d % 2]
                            TE.wait(ac, tp_ac4[d % 2])
                            TE.op("transpose", tb[:, 0:128],
                                  gbuf[:, gsl * D + d * 128: gsl * D + (d + 1) * 128],
                                  ident[:], inc=(pe, 1))
                            tp_pe4[d % 2] = pe.n
                            SC.wait(pe, tp_pe4[d % 2])
                            SC.op("activation",
                                  gfm[:, d * CAP + ct * 128: d * CAP + (ct + 1) * 128],
                                  tb[:, 0:128], AF.Identity)
                            tp_ac4[d % 2] = ac.n
                        gbuf_tp[gsl] = pe.n
                    gfm_ac = ac.n
                    gidx = 0
                    for so, sw in SEGS:
                        for ic in range(16):
                            bk_ = gsel[gidx % 2]
                            gidx += 1
                            TE.wait(ac, gfm_ac)
                            if gidx == 1:
                                TE.wait(s_w, wg_rdy)
                            TE.wait(bk_["ev"][0], bk_["ev"][1])
                            for d in range(16):
                                TE.op("matmul", bk_["ps"][:, 0:sw],
                                      Wbuf[:, d * D + ic * 128: d * D + (ic + 1) * 128],
                                      gfm[:, d * CAP + so: d * CAP + so + sw],
                                      start=(d == 0), stop=(d == 15),
                                      inc=(pe, 1) if d == 15 else None)
                            mm_done = pe.n
                            SC.wait(pe, mm_done)
                            SC.op("activation", gusp[:, ic * CAP + so: ic * CAP + so + sw],
                                  bk_["ps"][:, 0:sw], AF.Silu)
                            bk_["ev"] = (ac, ac.n)
                    pe_b4u = pe.n
                    for c in range(16):
                        GP.wait(pe, pe_b4u)
                        GP.dma(Wbuf[:, c * D:(c + 1) * D], WuB[e, c * 128:(c + 1) * 128, :], s_w)
                    wu_rdy = s_w.n
                    gidx = 0
                    for so, sw in SEGS:
                        for ic in range(16):
                            bk_ = gsel[gidx % 2]
                            gidx += 1
                            if gidx == 1:
                                TE.wait(s_w, wu_rdy)
                            TE.wait(bk_["ev"][0], bk_["ev"][1])
                            for d in range(16):
                                TE.op("matmul", bk_["ps"][:, 0:sw],
                                      Wbuf[:, d * D + ic * 128: d * D + (ic + 1) * 128],
                                      gfm[:, d * CAP + so: d * CAP + so + sw],
                                      start=(d == 0), stop=(d == 15),
                                      inc=(pe, 1) if d == 15 else None)
                            mm_done = pe.n
                            VE.wait(pe, mm_done)
                            VE.op("scalar_tensor_tensor",
                                  gusp[:, ic * CAP + so: ic * CAP + so + sw],
                                  gusp[:, ic * CAP + so: ic * CAP + so + sw],
                                  7.0, bk_["ps"][:, 0:sw], ALU.min, ALU.mult)
                            bk_["ev"] = (ve, ve.n)
                    pe_after_u = pe.n
                    first_d = True
                    for hg in range(4):
                        qsl = hg % 2
                        for c in range(16):
                            GP.wait(pe, wdq_user[qsl])
                            GP.dma(wdq[qsl][:, c * 512:(c + 1) * 512],
                                   WdB[e, c * 128:(c + 1) * 128, hg * 512:(hg + 1) * 512],
                                   s_wd[qsl])
                        wq_rdy[qsl] = s_wd[qsl].n
                        if hg == 0 and e + 1 < NEXP:
                            for c in range(16):
                                GP.wait(pe, pe_after_u)
                                GP.dma(Wbuf[:, c * D:(c + 1) * D],
                                       WgB[e + 1, c * 128:(c + 1) * 128, :], s_w)
                            wg_rdy = s_w.n
                        for ct in range(NCT):
                            bk_ = gsel[ct % 2]
                            hslot = ct % 2
                            sas = s_sa[hg * 4 + (ct % 2) * 2 + es]
                            TE.wait(bk_["ev"][0], bk_["ev"][1])
                            TE.wait(s_wd[qsl], wq_rdy[qsl])
                            if first_d:
                                TE.wait(ve)
                                first_d = False
                            for ic in range(16):
                                TE.op("matmul", bk_["ps"][:],
                                      gusp[:, ic * CAP + ct * 128: ic * CAP + (ct + 1) * 128],
                                      wdq[qsl][:, ic * 512:(ic + 1) * 512],
                                      start=(ic == 0), stop=(ic == 15),
                                      inc=(pe, 1) if ic == 15 else None)
                            mm_done = pe.n
                            if ct >= NCT - 2:
                                wdq_user[qsl] = mm_done
                            VE.wait(pe, mm_done)
                            if hsc_guard[hslot] is not None:
                                VE.wait(hsc_guard[hslot][0], hsc_guard[hslot][1])
                            VE.op("tensor_scalar_mul", hsc[:, hslot * 512:(hslot + 1) * 512],
                                  bk_["ps"][:], gcol[:, ct:ct + 1])
                            bk_["ev"] = (ve, ve.n)
                            gcol_user2[es] = ve.n
                            GP.wait(ve, bk_["ev"][1])
                            if first_accum[0]:
                                GP.wait(s_o)
                                first_accum[0] = False
                            if ct < 2:
                                for p_ in range(2):
                                    for ep_ in range(2):
                                        GP.wait(s_sa[hg * 4 + p_ * 2 + ep_])
                            GP.op("indirect_dma_start",
                                  out=out[:],
                                  out_offset=bass.IndirectOffsetOnAxis(ap=idxg[:, ct:ct + 1], axis=0),
                                  in_=hsc[:, hslot * 512:(hslot + 1) * 512],
                                  in_offset=None,
                                  element_offset=hg * 512,
                                  compute_op=ALU.add,
                                  inc=(sas, 16), chain=False)
                            hsc_guard[hslot] = (sas, sas.n)

        for d, src in dbg_outs:
            SY.wait(s_kv)
            SY.wait(s_ct)
            SY.dma(d, src, s_o)
        for s_ in s_sa:
            SY.wait(s_)
        SY.wait(s_o)
        SY.wait(s_kv)
        SY.wait(s_h2)
        SY.wait(s_tr)
        SY.wait(s_ct)

        with contextlib.ExitStack() as ss:
            for s in P.sems.values():
                s.h = ss.enter_context(nc.semaphore(s.name))
            with nc.Block() as block:
                @block.sync
                def _(e):
                    P.engs["sync"].replay(e)

                @block.tensor
                def _(e):
                    P.engs["tensor"].replay(e)

                @block.vector
                def _(e):
                    P.engs["vector"].replay(e)

                @block.scalar
                def _(e):
                    P.engs["scalar"].replay(e)

                @block.gpsimd
                def _(e):
                    P.engs["gpsimd"].replay(e)

    return nc



def _build_masks(cp):
    m = np.zeros((NCHUNK, 16, 128, QC), dtype=np.float32)
    for k in range(NCHUNK):
        span = (k + 1) * 2048
        q0 = k * 2048 + cp * QC
        kv = (span - 2048) + np.arange(16 * 128).reshape(16, 128)
        q = q0 + np.arange(QC)
        m[k] = (kv[:, :, None] <= q[None, None, :]).astype(np.float32)
    return m.astype(BF)


_NC_CACHE = {}


def _get_nc(key=("p1", "p2", "p3", "p4"), dbg=()):
    ck = (key, dbg)
    if ck not in _NC_CACHE:
        _NC_CACHE[ck] = build_nc(phases=key, dbg=dbg)
    return _NC_CACHE[ck]


def make_in_maps(x, Wq, bq, Wk, bk, Wv, bv, Wo, bo, Wr, Wg, Wu, Wd):
    x = np.asarray(x, dtype=np.float32)
    shared = dict(
        WqT=np.ascontiguousarray(np.asarray(Wq, np.float32).T),
        WkT=np.ascontiguousarray(np.asarray(Wk, np.float32).T),
        WvT=np.ascontiguousarray(np.asarray(Wv, np.float32).T),
        WoT=np.ascontiguousarray(np.asarray(Wo, np.float32).T),
        bq=np.asarray(bq, np.float32), bk=np.asarray(bk, np.float32),
        bv=np.asarray(bv, np.float32), bo=np.asarray(bo, np.float32),
        Wr=np.ascontiguousarray(np.asarray(Wr, np.float32)),
        Wg=np.ascontiguousarray(np.asarray(Wg, np.float32)),
        Wu=np.ascontiguousarray(np.asarray(Wu, np.float32)),
        Wd=np.ascontiguousarray(np.asarray(Wd, np.float32)),
    )
    in_maps = []
    for c in range(NCORE):
        b, cp = c // 4, c % 4
        rows = [x[b, k * 2048 + cp * QC: k * 2048 + (cp + 1) * QC] for k in range(NCHUNK)]
        in_maps.append(dict(shared,
                            xq=np.ascontiguousarray(np.concatenate(rows, 0)),
                            xkv=np.ascontiguousarray(x[b]),
                            masks=np.ascontiguousarray(_build_masks(cp)[0])))
    return in_maps


def assemble_out(results):
    outf = np.empty((NB, S, D), dtype=np.float32)
    for c in range(NCORE):
        b, cp = c // 4, c % 4
        o = results[c]["out"]
        for k in range(NCHUNK):
            outf[b, k * 2048 + cp * QC: k * 2048 + (cp + 1) * QC] = o[k * QC:(k + 1) * QC]
    return outf


def kernel(L0_attn_x, L0_attn_Wq, L0_attn_bq, L0_attn_Wk, L0_attn_bk,
           L0_attn_Wv, L0_attn_bv, L0_attn_Wo, L0_attn_bo,
           L0_ffn_Wr, L0_ffn_Wg, L0_ffn_Wu, L0_ffn_Wd):
    in_maps = make_in_maps(L0_attn_x, L0_attn_Wq, L0_attn_bq, L0_attn_Wk,
                           L0_attn_bk, L0_attn_Wv, L0_attn_bv, L0_attn_Wo,
                           L0_attn_bo, L0_ffn_Wr, L0_ffn_Wg, L0_ffn_Wu, L0_ffn_Wd)
    nc = _get_nc()
    res = run_bass_kernel_spmd(nc, in_maps, core_ids=list(range(NCORE)))
    return assemble_out(res.results)
